# version 12
# baseline (speedup 1.0000x reference)
import numpy as np
import ml_dtypes
import concourse.bass as bass
import concourse.mybir as mybir
from concourse.bass_utils import run_bass_kernel_spmd

F32 = mybir.dt.float32
BF16 = mybir.dt.bfloat16
I32 = mybir.dt.int32
AF = mybir.ActivationFunctionType
ALU = mybir.AluOpType
AX = mybir.AxisListType

B, T, D, H = 4, 4096, 1024, 16
NCORE = 8
TPC = 2048
DFF = 3072
NOC = DFF // 128
NDC = D // 128
NTT = TPC // 128
EPS = 1e-5
B_ACT = 32768.0
MAGIC = 12582912.0
B_W = 24.0
EXPC = 12.0
QMAXA = 32767.0
QMAXW = 7.0

_DBG_TAPS = []
_DBG_DRAM = []
_DBG_CELLS = [False]
_DBG_QKV = [False]
_DBG_SEQ = None
_RES_HOOK = [None]

AR_BYTES = 196608
OFF_CMASK = 0
OFF_IDENT = 8192
OFF_BOUT = 8448
OFF_CELLS = 12544
OFF_A = 13312
OFF_QKV = 37888
HP_STRIDE = 24704
def off_Q(hp): return OFF_QKV + hp * HP_STRIDE
def off_K(hp): return OFF_QKV + hp * HP_STRIDE + 8192
def off_V(hp): return OFF_QKV + hp * HP_STRIDE + 16384
OFF_O = 136704
OFF_PT0 = 169472
OFF_PT1 = 171520
OFF_RB64 = 173568
OFF_DENLN = 177664
OFF_OBT0 = 181760
OFF_OBT1 = 183808
OFF_SA = 185856
OFF_WDNAT = 189952
OFF_TAIL2 = 192000
OFF_REP2 = 196096
OFF_XST0 = OFF_O
OFF_XST1 = OFF_O + 16384
OFF_WQST = OFF_O
OFF_XF = 169472
OFF_XLN0 = 177664
OFF_XLN1 = 179712
OFF_WQBF = 13312
OFF_WKBF = 21504
OFF_WVBF = 29696
OFF_WUPT = 62592
OFF_WGT = 111744
AFULL_OFFS = [160896, 164992, 169472, 173568, 177664, 181760, 185856, 189952]
OFF_WOUTBF = 13312
OFF_YSB = 29696
OFF_XMIDT = 33792
OFF_XLN2T = 37888
OFF_XH0 = 41984
OFF_XH1 = 46080
XQ_OFF = 13312
OFF_XL2LD0 = 46080
OFF_XL2LD1 = 50176
OFF_XQI = 54272
OFF_XQBF = 58368
OFF_F1ST0 = 160896
OFF_F1ST1 = 164992
OFF_PGLD0 = 46080
OFF_PGLD1 = 50176
OFF_T1I = 54272
OFF_SIG = 13312
OFF_T2I = 21504
OFF_T1F = 29696
OFF_T3A = 160896
OFF_T3B = 169472
OFF_PULD0 = 46080
OFF_PULD1 = 50176
OFF_T3LD0 = 13312
OFF_T3LD1 = 21504
OFF_T4I = 29696
OFF_T5I = 54272
OFF_T5F = 37888
OFF_T6A = 160896
OFF_T6B = 169472
OFF_WDT = 13312
OFF_T7 = 62592
OFF_T6LD0 = 160896
OFF_T6LD1 = 169472
OFF_T7I = 177664
OFF_XMLD0 = 185856
OFF_XMLD1 = 189952
OFF_OUTB0 = 160896
OFF_OUTB1 = 164992

CELL_SCALE8, CELL_NEGC, CELL_EPS, CELL_INVSX, CELL_C1, CELL_SG, CELL_BSG, \
    CELL_C3, CELL_C4, CELL_C5, CELL_SO, CELL_SO2, CELL_ZERO, CELL_ONE, \
    CELL_NEGBA, CELL_MHALF = range(16)
CELL_NEGMG = 16

ROW_SX, ROW_SG, ROW_SU, ROW_S2, ROW_SO, ROW_TA, ROW_TB, ROW_P0, ROW_P1, ROW_P2 = \
    8, 9, 10, 11, 12, 13, 14, 20, 21, 22


def _dt_size(dt):
    return {F32: 4, BF16: 2, I32: 4}[dt]


def build_nc(taps=()):
    nc = bass.Bass("TRN2", target_bir_lowering=False, detect_race_conditions=False)

    xb = nc.dram_tensor("xb", [T, D], F32, kind="ExternalInput")
    xhalf = nc.dram_tensor("xhalf", [TPC, D], F32, kind="ExternalInput")
    wq_t = nc.dram_tensor("wq_t", [D, 512], F32, kind="ExternalInput")
    wk_t = nc.dram_tensor("wk_t", [D, 512], F32, kind="ExternalInput")
    wv_t = nc.dram_tensor("wv_t", [D, 512], F32, kind="ExternalInput")
    wout_t = nc.dram_tensor("wout_t", [D, D], F32, kind="ExternalInput")
    boutv = nc.dram_tensor("boutv", [D], F32, kind="ExternalInput")
    wg = nc.dram_tensor("wg", [DFF, D], F32, kind="ExternalInput")
    wu = nc.dram_tensor("wu", [DFF, D], F32, kind="ExternalInput")
    wd = nc.dram_tensor("wd", [D, DFF], F32, kind="ExternalInput")
    cmask2 = nc.dram_tensor("cmask2", [128, 4096], BF16, kind="ExternalInput")
    identb = nc.dram_tensor("identb", [128, 128], BF16, kind="ExternalInput")
    hselin = nc.dram_tensor("hselin", [128, 2], F32, kind="ExternalInput")
    exin = nc.dram_tensor("exin", [64, 1024], F32, kind="ExternalInput")
    out = nc.dram_tensor("out", [TPC, D], F32, kind="ExternalOutput")

    sp_pu = nc.dram_tensor("sp_pu", [NOC, 128, TPC], BF16)
    sp_pg = nc.dram_tensor("sp_pg", [NOC, 128, TPC], BF16)
    sp_t3 = nc.dram_tensor("sp_t3", [NOC, 128, TPC], F32)
    sp_t6 = nc.dram_tensor("sp_t6", [NOC, 128, TPC], F32)
    sp_xmid = nc.dram_tensor("sp_xmid", [NTT, 128, D], F32)
    sp_xln2 = nc.dram_tensor("sp_xln2", [NTT, 128, D], F32)
    sp_wd = nc.dram_tensor("sp_wd", [NOC, 128, D], BF16)
    cc_ex_in = [nc.dram_tensor(f"cc_exi{hp}", [128, T], BF16) for hp in range(4)]
    cc_ex_out = [nc.dram_tensor(f"cc_exo{hp}", [2, 128, T], BF16) for hp in range(4)]
    cc_b_in = [nc.dram_tensor(f"cc_bi{i}", [1, 8], F32) for i in range(4)]
    cc_b_out = [nc.dram_tensor(f"cc_bo{i}", [8, 8], F32) for i in range(4)]
    rb_d = nc.dram_tensor("rb_d", [1, 2048], F32)
    cdram = nc.dram_tensor("cdram", [1, 64], F32)
    dbg = {}
    for (name, off, dt, nel) in taps:
        dbg[name] = nc.dram_tensor("dbg_" + name, [128, nel], F32, kind="ExternalOutput")
    dram_map = {"sp_pu": (sp_pu, [NOC, 128, TPC], BF16), "sp_pg": (sp_pg, [NOC, 128, TPC], BF16),
                "sp_t3": (sp_t3, [NOC, 128, TPC], F32), "sp_t6": (sp_t6, [NOC, 128, TPC], F32),
                "sp_xmid": (sp_xmid, [NTT, 128, D], F32), "sp_xln2": (sp_xln2, [NTT, 128, D], F32),
                "sp_wd": (sp_wd, [NOC, 128, D], BF16),
                "cc_exo0": (cc_ex_out[0], [2, 128, T], BF16),
                "cc_bo0": (cc_b_out[0], [8, 8], F32), "cc_bo1": (cc_b_out[1], [8, 8], F32),
                "cc_bo2": (cc_b_out[2], [8, 8], F32), "cc_bo3": (cc_b_out[3], [8, 8], F32)}
    if _DBG_QKV[0]:
        dbg_q0 = nc.dram_tensor("dbg_q0", [128, 4096], BF16, kind="ExternalOutput")
        dbg_k0 = nc.dram_tensor("dbg_k0", [128, 4096], BF16, kind="ExternalOutput")
        dbg_v0 = nc.dram_tensor("dbg_v0", [128, 4160], BF16, kind="ExternalOutput")
        dbg_pqv = nc.dram_tensor("dbg_pqv", [128, 2048], F32, kind="ExternalOutput")
        dbg_xf = nc.dram_tensor("dbg_xf", [128, 4096], BF16, kind="ExternalOutput")
        dbg_wv = nc.dram_tensor("dbg_wv", [128, 4096], BF16, kind="ExternalOutput")
    if _DBG_CELLS[0]:
        dbg_cells = nc.dram_tensor("dbg_cells", [128, 64], F32, kind="ExternalOutput")
        dbg_row = nc.dram_tensor("dbg_row", [2, 64], F32, kind="ExternalOutput")
        dbg_small = nc.dram_tensor("dbg_small", [128, 8], F32, kind="ExternalOutput")
        dbg_dsb = nc.dram_tensor("dbg_dsb", [128, 2048], F32, kind="ExternalOutput")
    ddbg = {}
    for name in _DBG_DRAM:
        h, shp, dt = dram_map[name]
        ddbg[name] = (nc.dram_tensor("ddbg_" + name, shp, dt, kind="ExternalOutput"), h)

    AR = nc.alloc_sbuf_tensor("arena", [128, AR_BYTES], mybir.dt.uint8)

    def arv(off, dt, shape, p0=0):
        parts = shape[0]
        n = 1
        for s in shape[1:]:
            n *= s
        ap = AR[p0:p0 + parts, off:off + n * _dt_size(dt)].bitcast(dt)
        if len(shape) == 3:
            ap = ap.rearrange("p (a b) -> p a b", b=shape[2])
        elif len(shape) == 4:
            ap = ap.rearrange("p (a b c) -> p a b c", b=shape[2], c=shape[3])
        return ap

    sbt = nc.alloc_sbuf_tensor
    rowbuf = sbt("rowbuf", [1, 64], F32)
    rowcc = sbt("rowcc", [1, 64], F32)
    bn6 = sbt("bn6", [128, 12], F32)
    bn2 = sbt("bn2", [128, 2], F32)
    r1 = sbt("r1", [128, 1], F32)
    r1v = sbt("r1v", [128, 1], F32)
    nmrs = sbt("nmrs", [128, 1], F32)
    tmx = sbt("tmx", [128, 1], F32)
    par = sbt("par", [128, 1], F32)
    mx_x = sbt("mx_x", [128, 1], F32)
    mx_u = sbt("mx_u", [128, 1], F32)
    mx_g = sbt("mx_g", [128, 1], F32)
    mx_3 = sbt("mx_3", [128, 1], F32)
    mx_6 = sbt("mx_6", [128, 1], F32)
    wbmax = sbt("wbmax", [128, 32], F32)
    invpack = sbt("invpack", [128, 64], F32)
    expnd = sbt("expnd", [64, 1024], F32)
    sT_sb = sbt("sT_sb", [64, 128], F32)
    wbinv = invpack[:, 0:32]
    wbs = invpack[:, 32:64]
    hsel = sbt("hsel", [128, 2], F32)
    cellrow = sbt("cellrow", [1, 64], F32)
    identf = sbt("identf", [128, 128], F32)
    zero_cell = sbt("zero_cell", [128, 1], F32)
    one_cell = sbt("one_cell", [128, 1], F32)
    zero_cb = sbt("zero_cb", [128, 1], BF16)
    one_cb = sbt("one_cb", [128, 1], BF16)

    cells = sbt("cells", [128, 64], F32)
    cmask_sb = arv(OFF_CMASK, BF16, [128, 4, 1024])
    ident_sb = arv(OFF_IDENT, BF16, [128, 128])
    bout_rep = arv(OFF_BOUT, F32, [128, 1024])
    boutrow = arv(OFF_BOUT, F32, [1, 1024])

    pq = nc.alloc_psum_tensor("pq", [128, 2048], F32)
    pst = nc.alloc_psum_tensor("pst", [128, 1024], F32)
    pmA = nc.alloc_psum_tensor("pmA", [128, 512], BF16)
    pmR = nc.alloc_psum_tensor("pmR", [128, 512], F32)

    ops = {e: [] for e in ("sync", "tensor", "vector", "scalar", "gpsimd")}
    cnt = {}
    sems = {}
    waited = {e: {} for e in ops}
    from contextlib import ExitStack
    es = ExitStack()
    for name in ("ld", "l2", "sp", "tm", "vx", "sc", "gp", "cc", "st"):
        sems[name] = es.enter_context(nc.semaphore(name))
        cnt[name] = 0

    seqc = [0]

    import traceback as _tb

    def _app(eng, fn):
        seqc[0] += 1
        if _DBG_SEQ and _DBG_SEQ[0] <= seqc[0] <= _DBG_SEQ[1]:
            fr = _tb.extract_stack()[-3]
            print(f"seq {seqc[0]} [{eng}] {fr.name}:{fr.lineno} {fr.line[:90] if fr.line else ''}")
        ops[eng].append((seqc[0], fn))

    def NOTICK(eng, fn):
        _app(eng, fn)

    def TICK(eng, fn, sem, k=1):
        cnt[sem] += k
        h = sems[sem]
        _app(eng, lambda e, fn=fn, h=h, k=k: fn(e).then_inc(h, k))
        return cnt[sem]

    def WAIT(eng, sem, v):
        if not v or v <= 0:
            return
        if waited[eng].get(sem, 0) >= v:
            return
        waited[eng][sem] = v
        h = sems[sem]
        _app(eng, lambda e, h=h, v=v: e.wait_ge(h, v))

    nc.const_aps.register(F32, 0.0, zero_cell[:, :])
    nc.const_aps.register(F32, 1.0, one_cell[:, :])
    nc.const_aps.register(BF16, 0.0, zero_cb[:, :])
    nc.const_aps.register(BF16, 1.0, one_cb[:, :])

    CL = lambda i: cells[:, i:i + 1]

    def ln_rstd(var_ap):
        NOTICK("scalar", lambda e: e.activation(r1[:, :], var_ap, AF.Ln, bias=CL(CELL_EPS)))
        return TICK("scalar", lambda e: e.activation(r1[:, :], r1[:, :], AF.Exp, scale=CL(CELL_MHALF)), "sc")

    def ln_tail(var_ap, mean_ap):
        st_ = ln_rstd(var_ap)
        WAIT("vector", "sc", st_)
        NOTICK("vector", lambda e: e.tensor_copy(r1v[:, :], r1[:, :]))
        vt2 = TICK("vector", lambda e, m=mean_ap: e.scalar_tensor_tensor(
            nmrs[:, :], m, -1.0, r1[:, :], ALU.mult, ALU.mult), "vx")
        WAIT("scalar", "vx", vt2)
        return vt2

    def p0():
        for cell, val in ((zero_cell, 0.0), (one_cell, 1.0)):
            NOTICK("vector", lambda e, c=cell, v=val: e.memset(c[:, :], v))
        for cell, val in ((zero_cb, 0.0), (one_cb, 1.0)):
            NOTICK("vector", lambda e, c=cell, v=val: e.memset(c[:, :], v))
        NOTICK("vector", lambda e: e.memset(cells[:, :], 0.0))
        for idx, val in ((CELL_SCALE8, 0.125), (CELL_NEGC, -EXPC), (CELL_EPS, EPS),
                         (CELL_ONE, 1.0), (CELL_NEGBA, -B_ACT), (CELL_MHALF, -0.5),
                         (CELL_NEGMG, -MAGIC)):
            NOTICK("vector", lambda e, i=idx, v=val: e.memset(CL(i), v))
        for m in (mx_x, mx_u, mx_g, mx_3, mx_6):
            NOTICK("vector", lambda e, m=m: e.memset(m[:, :], 0.0))
        for hp in range(4):
            vV = arv(off_V(hp), BF16, [128, 4160])
            NOTICK("vector", lambda e, a=vV: e.memset(a, 1.0))
        v0 = TICK("vector", lambda e: e.memset(rowbuf[:, :], 0.0), "vx")

        TICK("sync", lambda e: e.dma_start(out=cmask_sb, in_=cmask2.ap().rearrange("p (a b) -> p a b", b=1024)), "ld", 16)
        ti = TICK("sync", lambda e: e.dma_start(out=ident_sb, in_=identb.ap()), "ld", 16)
        TICK("sync", lambda e: e.dma_start(
            out=bout_rep, in_=boutv.ap().unsqueeze(0).partition_broadcast(128).opt()), "ld", 16)
        t4 = TICK("sync", lambda e: e.dma_start(out=hsel[:, :], in_=hselin.ap()), "ld", 16)
        TICK("sync", lambda e: e.dma_start(out=expnd[:, :], in_=exin.ap()), "ld", 16)
        WAIT("vector", "ld", ti)
        TICK("vector", lambda e: e.tensor_copy(identf[:, :], ident_sb), "vx")
        return t4, v0

    def p1():
        wst = arv(OFF_WQST, F32, [128, 4096])
        wst3 = arv(OFF_WQST, F32, [128, 8, 512])
        cast_t = 0
        for dram, woff in ((wq_t, OFF_WQBF), (wk_t, OFF_WKBF), (wv_t, OFF_WVBF)):
            WAIT("sync", "vx", cast_t)
            lt = TICK("sync", lambda e, d=dram: e.dma_start(
                out=wst3, in_=d.ap().rearrange("(c p) o -> p c o", p=128)), "ld", 16)
            WAIT("vector", "ld", lt)
            wbf = arv(woff, BF16, [128, 4096])
            cast_t = TICK("vector", lambda e, a=wbf: e.tensor_copy(a, wst), "vx")
        wqbf = arv(OFF_WQBF, BF16, [128, 8, 512])
        wkbf = arv(OFF_WKBF, BF16, [128, 8, 512])
        wvbf = arv(OFF_WVBF, BF16, [128, 8, 512])

        xst = [arv(OFF_XST0, F32, [128, 4, 1024]), arv(OFF_XST1, F32, [128, 4, 1024])]
        xln = [arv(OFF_XLN0, BF16, [128, 1024]), arv(OFF_XLN1, BF16, [128, 1024])]
        xf = arv(OFF_XF, BF16, [128, 8, 512])

        consumed = [("vx", cast_t), ("sc", 0)]
        pm_ev = [0, 0]
        slot_ev = {}
        prev_tile_mm = 0
        trt = 0
        for tt in range(8):
            buf = tt % 2
            WAIT("sync", consumed[buf][0], consumed[buf][1])
            lt = TICK("sync", lambda e, tt=tt, b=buf: e.dma_start(
                out=xst[b],
                in_=xb.ap()[tt * 512:(tt + 1) * 512, :].rearrange("(a p) d -> p a d", p=128)),
                "ld", 16)
            xf_ev = 0
            for a in range(4):
                xa = xst[buf][:, a, :]
                WAIT("vector", "ld", lt)
                NOTICK("vector", lambda e, x=xa: e.bn_stats(bn6[:, 0:6], x[:, 0:512]))
                vsync(lambda e, x=xa: e.bn_stats(bn6[:, 6:12], x[:, 512:1024]))
                vt = TICK("vector", lambda e: e.bn_aggr(bn2[:, :], bn6[:, :]), "vx")
                WAIT("scalar", "vx", vt)
                ln_tail(bn2[:, 1:2], bn2[:, 0:1])
                lnt = TICK("scalar", lambda e, x=xa, o=xln[a % 2]: e.activation(
                    o, x, AF.Identity, bias=nmrs[:, :], scale=r1v[:, :]), "sc")
                if a == 3:
                    consumed[buf] = ("sc", lnt)
                WAIT("tensor", "sc", lnt)
                for grp in range(2):
                    WAIT("tensor", "vx", pm_ev[0])
                    for ic in range(4):
                        fn = (lambda e, i=grp * 4 + ic, j=ic, o=xln[a % 2]:
                              e.transpose(pmA[:, j * 128:(j + 1) * 128], o[:, i * 128:(i + 1) * 128], ident_sb))
                        trt = TICK("tensor", fn, "tm") if ic == 3 else NOTICK("tensor", fn) or trt
                    WAIT("vector", "tm", trt)
                    if a == 0 and grp == 0:
                        WAIT("vector", "tm", prev_tile_mm)
                    pm_ev[0] = TICK("vector", lambda e, a_=a, g=grp: e.tensor_copy(
                        xf[:, g * 4:(g + 1) * 4, a_ * 128:(a_ + 1) * 128],
                        pmA[:, :].rearrange("p (a b) -> p a b", b=128)), "vx")
                xf_ev = pm_ev[0]
            WAIT("tensor", "vx", xf_ev)
            for kind in ("q", "k"):
                wbf_ = wqbf if kind == "q" else wkbf
                for oc in range(4):
                    sem_o, val_o = slot_ev.get(oc, ("sc", 0))
                    WAIT("tensor", sem_o, val_o)
                    for dc in range(NDC):
                        fn = (lambda e, w=wbf_, dc=dc, oc=oc:
                              e.matmul(pq[:, oc * 512:(oc + 1) * 512], w[:, dc, oc * 128:(oc + 1) * 128],
                                       xf[:, dc, :], start=(dc == 0), stop=(dc == NDC - 1)))
                        mmt = TICK("tensor", fn, "tm") if dc == NDC - 1 else NOTICK("tensor", fn) or 0
                    WAIT("scalar", "tm", mmt)
                    dst = arv(off_Q(oc) if kind == "q" else off_K(oc), BF16, [128, 4096])
                    slot_ev[oc] = ("sc", TICK("scalar", lambda e, d=dst, oc=oc, tt=tt: e.copy(
                        d[:, tt * 512:(tt + 1) * 512], pq[:, oc * 512:(oc + 1) * 512]), "sc"))
                    prev_tile_mm = mmt
            for a in range(4):
                sem_a, val_a = slot_ev.get(a, ("sc", 0))
                WAIT("tensor", sem_a, val_a)
                for dc in range(NDC):
                    fn = (lambda e, dc=dc, a=a:
                          e.matmul(pq[:, a * 512:(a + 1) * 512], xf[:, dc, a * 128:(a + 1) * 128],
                                   wvbf[:, dc, :], start=(dc == 0), stop=(dc == NDC - 1)))
                    mmt = TICK("tensor", fn, "tm") if dc == NDC - 1 else NOTICK("tensor", fn) or 0
                WAIT("vector", "tm", mmt)
                kb = tt * 4 + a
                last_v = 0
                for hp in range(4):
                    fn = (lambda e, hp=hp, a=a, kb=kb: e.tensor_copy(
                        arv(off_V(hp), BF16, [128, 2, 32, 65])[:, :, kb, 0:64],
                        pq[:, a * 512:(a + 1) * 512].rearrange("p (h d) -> p h d", d=64)[:, 2 * hp:2 * hp + 2, :]))
                    if hp == 3:
                        last_v = TICK("vector", fn, "vx")
                        slot_ev[a] = ("vx", last_v)
                    else:
                        NOTICK("vector", fn)
                if _DBG_QKV[0] and tt == 7:
                    stv = arv(62592, F32, [128, 2048])
                    last_v = TICK("vector", lambda e, a=a: e.tensor_copy(
                        stv[:, a * 512:(a + 1) * 512], pq[:, a * 512:(a + 1) * 512]), "vx")
                    slot_ev[a] = ("vx", last_v)
                prev_tile_mm = mmt
        last_vx = max(pm_ev)
        return ("vx", last_v), last_vx, prev_tile_mm

    dq_state = {"sa_free": 0, "pm_free": [0, 0], "pmi": 0, "wnat_free": 0,
                "last_vx": 0, "last_tm": 0, "last_sp": 0, "f1st_free": 0}

    def emit_dequant_chunk(kind, oc):
        sa = arv(OFF_SA, F32, [128, 1024])
        wnat = arv(OFF_WDNAT, BF16, [128, 1024])
        tmpf = arv(OFF_TAIL2, F32, [128, 1024])
        ti = arv(OFF_TAIL2, F32, [128, 1024])
        WAIT("scalar", "vx", dq_state["sa_free"])
        if kind == "wd":
            r0, c0 = (oc // 3) * 128, (oc % 3) * 1024
            lt = TICK("scalar", lambda e, r0=r0, c0=c0: e.dma_start(
                out=sa, in_=wd.ap()[r0:r0 + 128, c0:c0 + 1024]), "l2", 16)
        else:
            dram = wu if kind == "wu" else wg
            lt = TICK("scalar", lambda e, d=dram, oc=oc: e.dma_start(
                out=sa, in_=d.ap()[oc * 128:(oc + 1) * 128, :]), "l2", 16)
        WAIT("vector", "l2", lt)
        WAIT("vector", "tm", dq_state["wnat_free"])
        WAIT("vector", "tm", dq_state.get("ipk_free", 0))
        sa3 = sa.rearrange("p (a b) -> p a b", b=32)
        vsync(lambda e: e.reduce_max(wbmax[:, :], sa3, axis=AX.X, apply_absolute_value=True))
        vsync(lambda e: e.tensor_scalar(wbs, wbmax[:, :], 1.0 / QMAXW, 1e-12, ALU.mult, ALU.max))
        ivt = vsync(lambda e: e.reciprocal(wbinv, wbs))
        WAIT("tensor", "vx", ivt)
        WAIT("tensor", "vx", dq_state.get("pmr_free", 0))
        trt0 = TICK("tensor", lambda e: e.transpose(pmR[0:64, 0:128], invpack[:, :], identf[:, :]), "tm")
        dq_state["ipk_free"] = trt0
        WAIT("vector", "tm", trt0)
        evt = TICK("vector", lambda e: e.tensor_copy(sT_sb[:, :], pmR[0:64, 0:128]), "vx")
        prev_v = evt
        for half in range(2):
            WAIT("tensor", "vx", prev_v)
            mmt = TICK("tensor", lambda e, h=half: e.matmul(
                pmR[:, :], sT_sb[0:32, :], expnd[0:32, h * 512:(h + 1) * 512],
                start=True, stop=True), "tm")
            WAIT("vector", "tm", mmt)
            prev_v = TICK("vector", lambda e, h=half: e.tensor_tensor(
                tmpf[:, h * 512:(h + 1) * 512], sa[:, h * 512:(h + 1) * 512],
                pmR[:, :], ALU.mult), "vx")
        sa_f = TICK("vector", lambda e: e.tensor_scalar(ti, tmpf, MAGIC, None, ALU.add), "vx")
        dq_state["sa_free"] = sa_f
        prev_v = sa_f
        for half in range(2):
            WAIT("tensor", "vx", prev_v)
            mmt = TICK("tensor", lambda e, h=half: e.matmul(
                pmR[:, :], sT_sb[32:64, :], expnd[32:64, h * 512:(h + 1) * 512],
                start=True, stop=True, tile_position=(32, 0)), "tm")
            WAIT("vector", "tm", mmt)
            prev_v = TICK("vector", lambda e, h=half: e.scalar_tensor_tensor(
                wnat[:, h * 512:(h + 1) * 512], ti[:, h * 512:(h + 1) * 512],
                -MAGIC, pmR[:, :], ALU.add, ALU.mult), "vx")
        dq_state["pmr_free"] = prev_v
        vt2 = prev_v
        dq_state["last_vx"] = vt2
        trts = []
        for grp in range(2):
            WAIT("tensor", "vx", vt2)
            WAIT("tensor", "vx", dq_state["pm_free"][0])
            trt = 0
            for ic in range(4):
                icg = grp * 4 + ic
                fn = (lambda e, i=icg, j=ic: e.transpose(
                    pmA[:, j * 128:(j + 1) * 128], wnat[:, i * 128:(i + 1) * 128], ident_sb))
                trt = TICK("tensor", fn, "tm") if ic == 3 else NOTICK("tensor", fn) or trt
            dq_state["wnat_free"] = trt
            dq_state["last_tm"] = trt
            WAIT("vector", "tm", trt)
            if kind in ("wu", "wg"):
                tgt = arv(OFF_WUPT if kind == "wu" else OFF_WGT, BF16, [128, 8, 3072])
                ev = TICK("vector", lambda e, t=tgt, oc=oc, g=grp: e.tensor_copy(
                    t[:, g * 4:(g + 1) * 4, oc * 128:(oc + 1) * 128],
                    pmA[:, :].rearrange("p (a b) -> p a b", b=128)), "vx")
                dq_state["pm_free"][0] = ev
                dq_state["last_vx"] = ev
            else:
                stg = arv(OFF_A, BF16, [128, 1024])
                WAIT("vector", "sp", dq_state["f1st_free"] if grp == 0 else 0)
                ev = TICK("vector", lambda e, s=stg, g=grp: e.tensor_copy(
                    s[:, g * 512:(g + 1) * 512], pmA[:, :]), "vx")
                dq_state["pm_free"][0] = ev
                dq_state["last_vx"] = ev
        if kind == "wd":
            stg = arv(OFF_A, BF16, [128, 1024])
            WAIT("gpsimd", "vx", dq_state["last_vx"])
            icg, ob = (oc % 3) * 8, oc // 3
            spt = TICK("gpsimd", lambda e, s=stg, icg=icg, ob=ob: e.dma_start(
                out=sp_wd.ap()[icg:icg + 8, :, ob * 128:(ob + 1) * 128].transpose([1, 0, 2]),
                in_=s.rearrange("p (a b) -> p a b", b=128)), "sp", 16)
            dq_state["f1st_free"] = spt
            dq_state["last_sp"] = spt

    def p2(qkv_sc, qkv_vx, qkv_tm):
        units = [("wd", oc) for oc in range(24)] + \
                [("wu", oc) for oc in range(NOC)] + \
                [("wg", oc) for oc in range(NOC)]
        dq_iter = iter(units)
        dq_after = {}
        for i in range(1, 11):
            dq_after[i] = 2
        dq_after[11] = 4
        for i in range(12, 16):
            dq_after[i] = 6
        dq_state["pm_free"] = [qkv_vx, qkv_vx]

        PT = [arv(OFF_PT0, BF16, [128, 1024]), arv(OFF_PT1, BF16, [128, 1024])]
        rb64 = arv(OFF_RB64, F32, [64, 2048])
        denln = arv(OFF_DENLN, F32, [1, 2048], p0=64)
        OBT = [arv(OFF_OBT0, BF16, [64, 1024]), arv(OFF_OBT1, BF16, [64, 1024])]
        Ov = arv(OFF_O, BF16, [128, 4, 4096])

        pt_cons = [0, 0]
        pst_free, pst_sem = 0, "sc"
        acc_sem, acc_free = qkv_sc
        obt_mv = [0, 0]
        exp_i = 0
        cc_ticks = []
        blk = 0
        avt = 0
        for hp in range(4):
            Kv = arv(off_K(hp), BF16, [128, 4096])
            Qv = arv(off_Q(hp), BF16, [128, 4096])
            for g in range(4):
                qts = (2 * g, 2 * g + 1)
                for kb in range(4 * qts[1] + 4):
                    for qi, qt in enumerate(qts):
                        if kb > 4 * qt + 3:
                            continue
                        WAIT("tensor", pst_sem, pst_free)
                        stt = 0
                        for hi in range(2):
                            fn = (lambda e, hi=hi, kb=kb, qt=qt, Kv=Kv, Qv=Qv:
                                  e.matmul(pst[:, hi * 512:(hi + 1) * 512],
                                           Kv[hi * 64:(hi + 1) * 64, kb * 128:(kb + 1) * 128],
                                           Qv[hi * 64:(hi + 1) * 64, qt * 512:(qt + 1) * 512],
                                           start=True, stop=True, tile_position=(hi * 64, 0)))
                            stt = TICK("tensor", fn, "tm") if hi == 1 else NOTICK("tensor", fn) or stt
                        pp = exp_i % 2
                        exp_i += 1
                        WAIT("scalar", "tm", stt)
                        WAIT("scalar", "tm", pt_cons[pp])
                        ext = TICK("scalar", lambda e, pp=pp: e.activation(
                            PT[pp], pst[:, :], AF.Exp, scale=CL(CELL_SCALE8), bias=CL(CELL_NEGC)), "sc")
                        pst_free, pst_sem = ext, "sc"
                        rdy_sem, rdy = "sc", ext
                        if kb >= 4 * qt:
                            off = kb - 4 * qt
                            WAIT("vector", "sc", ext)
                            mkt = TICK("vector", lambda e, pp=pp, off=off: e.tensor_tensor(
                                PT[pp], PT[pp], cmask_sb[:, off, :], ALU.mult), "vx")
                            rdy_sem, rdy = "vx", mkt
                        WAIT("tensor", rdy_sem, rdy)
                        if kb == 0:
                            WAIT("tensor", acc_sem, acc_free)
                        for hi in range(2):
                            slot = qi if hi == 0 else 2 + qi
                            vstat = arv(off_V(hp), BF16, [128, 2, 32, 65])[:, hi, kb, :]
                            fn = (lambda e, hi=hi, pp=pp, slot=slot, vs=vstat, kb=kb, qt=qt:
                                  e.matmul(pq[0:65, slot * 512:(slot + 1) * 512], vs,
                                           PT[pp][:, hi * 512:(hi + 1) * 512],
                                           start=(kb == 0), stop=(kb == 4 * qt + 3)))
                            avt = TICK("tensor", fn, "tm") if hi == 1 else NOTICK("tensor", fn) or avt
                        pt_cons[pp] = avt
                WAIT("scalar", "tm", avt)
                NOTICK("scalar", lambda e: e.activation(denln, pq[64:65, :], AF.Ln))
                det = TICK("scalar", lambda e: e.activation(denln, denln, AF.Exp, scale=-1.0), "sc")
                WAIT("sync", "sc", det)
                WAIT("sync", acc_sem, acc_free)
                wt_ = TICK("sync", lambda e: e.dma_start(out=rb_d.ap(), in_=denln), "ld", 16)
                WAIT("sync", "ld", wt_)
                gbt = TICK("sync", lambda e: e.dma_start(
                    out=rb64, in_=rb_d.ap().partition_broadcast(64).opt()), "ld", 16)
                obi = g % 2
                WAIT("vector", "ld", gbt)
                WAIT("vector", "sp", obt_mv[obi])
                NOTICK("vector", lambda e, hp=hp, g=g: e.tensor_tensor(
                    Ov[0:64, hp, g * 1024:(g + 1) * 1024], pq[0:64, 0:1024], rb64[:, 0:1024], ALU.mult))
                acc_free = TICK("vector", lambda e, obi=obi: e.tensor_tensor(
                    OBT[obi], pq[0:64, 1024:2048], rb64[:, 1024:2048], ALU.mult), "vx")
                acc_sem = "vx"
                WAIT("gpsimd", "vx", acc_free)
                obt_mv[obi] = TICK("gpsimd", lambda e, hp=hp, g=g, obi=obi: e.dma_start(
                    out=Ov[64:128, hp, g * 1024:(g + 1) * 1024], in_=OBT[obi]), "sp", 16)
                for _ in range(dq_after.get(blk, 0)):
                    u = next(dq_iter, None)
                    if u:
                        emit_dequant_chunk(*u)
                blk += 1
            WAIT("sync", "vx", acc_free)
            WAIT("sync", "sp", obt_mv[0])
            WAIT("sync", "sp", obt_mv[1])
            sndt = TICK("sync", lambda e, hp=hp: e.dma_start(
                out=cc_ex_in[hp].ap(), in_=Ov[:, hp, :]), "ld", 16)
            WAIT("gpsimd", "ld", sndt)
            cct = TICK("gpsimd", lambda e, hp=hp: e.collective_compute(
                "AllGather", ALU.bypass,
                replica_groups=[[0, 1], [2, 3], [4, 5], [6, 7]],
                ins=[cc_ex_in[hp].ap().opt()],
                outs=[cc_ex_out[hp].ap().opt()]), "cc")
            cc_ticks.append(cct)
        WAIT("vector", "ld", sndt)
        WAIT("tensor", "ld", sndt)
        for u in dq_iter:
            emit_dequant_chunk(*u)
        return cc_ticks

    def barrier(bi, pay, vx_ready):
        vt = 0
        WAIT("tensor", "vx", vx_ready)
        trt = 0
        for i, (slot, mx) in enumerate(pay):
            trt = TICK("tensor", lambda e, m=mx, i=i: e.transpose(
                pst[0:1, i * 128:(i + 1) * 128], m[:, :], identf[:, :]), "tm")
        for i, (slot, mx) in enumerate(pay):
            WAIT("vector", "tm", trt)
            vt = TICK("vector", lambda e, s=slot, i=i: e.reduce_max(
                rowbuf[0:1, s:s + 1], pst[0:1, i * 128:(i + 1) * 128], axis=AX.X), "vx")
        WAIT("sync", "vx", vt)
        lt = TICK("sync", lambda e, bi=bi: e.dma_start(out=cc_b_in[bi].ap(), in_=rowbuf[0:1, 0:8]), "ld", 16)
        WAIT("gpsimd", "ld", lt)
        cct = TICK("gpsimd", lambda e, bi=bi: e.collective_compute(
            "AllGather", ALU.bypass, replica_groups=[list(range(8))],
            ins=[cc_b_in[bi].ap().opt()], outs=[cc_b_out[bi].ap().opt()]), "cc")
        WAIT("sync", "cc", cct)
        lt2 = TICK("sync", lambda e, bi=bi: e.dma_start(
            out=rowcc[0:1, :], in_=cc_b_out[bi].ap().rearrange("a b -> (a b)").unsqueeze(0)), "ld", 16)
        WAIT("vector", "ld", lt2)
        vt = 0
        for slot, _ in pay:
            rview = rowcc[0:1, :].rearrange("p (r s) -> p r s", s=8)[:, :, slot]
            vt = TICK("vector", lambda e, s=slot, rv=rview: e.reduce_max(
                rowbuf[0:1, s:s + 1], rv, axis=AX.X), "vx")
            WAIT("vector", "vx", vt)
        WAIT("sync", "vx", vt)
        wt2 = TICK("sync", lambda e: e.dma_start(out=cdram.ap()[0:1, 0:8], in_=rowbuf[0:1, 0:8]), "ld", 16)
        WAIT("sync", "ld", wt2)
        bt2 = TICK("sync", lambda e: e.dma_start(
            out=cells[:, 24:32],
            in_=cdram.ap()[0:1, 0:8].partition_broadcast(128).opt()), "ld", 16)
        WAIT("vector", "ld", bt2)

    def rowop(fn):
        v = TICK("vector", fn, "vx")
        WAIT("vector", "vx", v)
        return v

    def vsync(fn):
        v = TICK("vector", fn, "vx")
        WAIT("vector", "vx", v)
        return v

    def bcast_cells(lo, hi, vx_ready):
        WAIT("sync", "vx", vx_ready)
        wt_ = TICK("sync", lambda e, lo=lo, hi=hi: e.dma_start(
            out=cdram.ap()[0:1, lo:hi], in_=cellrow[0:1, lo:hi]), "ld", 16)
        WAIT("sync", "ld", wt_)
        return TICK("sync", lambda e, lo=lo, hi=hi: e.dma_start(
            out=cells[:, lo:hi],
            in_=cdram.ap()[0:1, lo:hi].partition_broadcast(128).opt()), "ld", 16)

    def p3(cc_ticks):
        Afull = [arv(AFULL_OFFS[c], BF16, [128, 2048]) for c in range(8)]
        woutbf = arv(OFF_WOUTBF, BF16, [128, 8, 1024])
        wst = arv(OFF_YSB, F32, [128, 1024])
        ysb = arv(OFF_YSB, F32, [128, 1024])
        xmidt = arv(OFF_XMIDT, F32, [128, 1024])
        xln2t = arv(OFF_XLN2T, F32, [128, 1024])
        xh = [arv(OFF_XH0, F32, [128, 1024]), arv(OFF_XH1, F32, [128, 1024])]

        WAIT("sync", "vx", dq_state["last_vx"])
        WAIT("sync", "tm", dq_state["last_tm"])
        WAIT("sync", "sp", dq_state["last_sp"])
        WAIT("sync", "l2", dq_state.get("invpack_free", 0))
        atmp = arv(OFF_XH0, BF16, [128, 2048])
        af_ld = 0
        blend_prev = 0
        for c in range(8):
            slot, hp = c // 4, c % 4
            WAIT("sync", "cc", cc_ticks[hp])
            WAIT("sync", "vx", blend_prev)
            lo_t = TICK("sync", lambda e, c=c, slot=slot, hp=hp: e.dma_start(
                out=Afull[c], in_=cc_ex_out[hp].ap()[slot, :, 0:TPC]), "ld", 16)
            hi_t = TICK("sync", lambda e, c=c, slot=slot, hp=hp: e.dma_start(
                out=atmp, in_=cc_ex_out[hp].ap()[slot, :, TPC:2 * TPC]), "ld", 16)
            WAIT("vector", "ld", hi_t)
            NOTICK("vector", lambda e, c=c: e.tensor_scalar(
                Afull[c], Afull[c], hsel[:, 0:1], None, ALU.mult))
            blend_prev = TICK("vector", lambda e, c=c: e.scalar_tensor_tensor(
                Afull[c], atmp, hsel[:, 1:2], Afull[c], ALU.mult, ALU.add), "vx")
            af_ld = blend_prev
        cast_t = 0
        for dc in range(8):
            WAIT("sync", "vx", cast_t)
            lt = TICK("sync", lambda e, dc=dc: e.dma_start(
                out=wst, in_=wout_t.ap()[dc * 128:(dc + 1) * 128, :]), "ld", 16)
            WAIT("vector", "ld", lt)
            cast_t = TICK("vector", lambda e, dc=dc: e.tensor_copy(woutbf[:, dc, :], wst), "vx")

        WAIT("tensor", "vx", af_ld)
        WAIT("tensor", "vx", cast_t)
        spill_t = 0
        ysb_sem, ysb_free = "vx", cast_t
        mm_free = [("vx", 0), ("vx", 0)]
        xh_free = [0, 0]
        amx = 0
        for tt in range(NTT):
            pb = tt % 2
            psv = pst if pb == 0 else pq
            WAIT("sync", "vx", xh_free[pb])
            xht = TICK("sync", lambda e, tt=tt, b=pb: e.dma_start(
                out=xh[b], in_=xhalf.ap()[tt * 128:(tt + 1) * 128, :]), "ld", 16)
            WAIT("tensor", mm_free[pb][0], mm_free[pb][1])
            mmt = 0
            for oh in range(2):
                for dc in range(8):
                    fn = (lambda e, ps=psv, oh=oh, dc=dc, tt=tt:
                          e.matmul(ps[:, oh * 512:(oh + 1) * 512],
                                   Afull[dc][:, tt * 128:(tt + 1) * 128],
                                   woutbf[:, dc, oh * 512:(oh + 1) * 512],
                                   start=(dc == 0), stop=(dc == 7)))
                    mmt = TICK("tensor", fn, "tm") if (dc == 7 and oh == 1) else NOTICK("tensor", fn) or mmt
            WAIT("vector", "tm", mmt)
            WAIT("vector", ysb_sem, ysb_free)
            vt = TICK("vector", lambda e, ps=psv: e.tensor_tensor(ysb, ps[:, 0:1024], bout_rep, ALU.add), "vx")
            mm_free[pb] = ("vx", vt)
            NOTICK("vector", lambda e: e.bn_stats(bn6[:, 0:6], ysb[:, 0:512]))
            vsync(lambda e: e.bn_stats(bn6[:, 6:12], ysb[:, 512:1024]))
            vt = TICK("vector", lambda e: e.bn_aggr(bn2[:, :], bn6[:, :]), "vx")
            WAIT("scalar", "vx", vt)
            ln_tail(bn2[:, 1:2], bn2[:, 0:1])
            st_ = TICK("scalar", lambda e: e.activation(ysb, ysb, AF.Identity, bias=nmrs[:, :], scale=r1v[:, :]), "sc")
            WAIT("vector", "sc", st_)
            WAIT("vector", "ld", xht)
            WAIT("vector", "sp", spill_t)
            xh_free[pb] = TICK("vector", lambda e, b=pb: e.tensor_tensor(xmidt, ysb, xh[b], ALU.add), "vx")
            ysb_sem, ysb_free = "vx", xh_free[pb]
            NOTICK("vector", lambda e: e.bn_stats(bn6[:, 0:6], xmidt[:, 0:512]))
            vsync(lambda e: e.bn_stats(bn6[:, 6:12], xmidt[:, 512:1024]))
            vt = TICK("vector", lambda e: e.bn_aggr(bn2[:, :], bn6[:, :]), "vx")
            WAIT("scalar", "vx", vt)
            ln_tail(bn2[:, 1:2], bn2[:, 0:1])
            WAIT("scalar", "sp", spill_t)
            st_ = TICK("scalar", lambda e: e.activation(xln2t, xmidt, AF.Identity, bias=nmrs[:, :], scale=r1v[:, :]), "sc")
            WAIT("vector", "sc", st_)
            vsync(lambda e: e.reduce_max(tmx[:, :], xln2t, axis=AX.X, apply_absolute_value=True))
            amx = TICK("vector", lambda e: e.tensor_tensor(mx_x[:, :], mx_x[:, :], tmx[:, :], ALU.max), "vx")
            WAIT("gpsimd", "vx", amx)
            WAIT("gpsimd", "sc", st_)
            TICK("gpsimd", lambda e, tt=tt: e.dma_start(out=sp_xmid.ap()[tt], in_=xmidt), "sp", 16)
            spill_t = TICK("gpsimd", lambda e, tt=tt: e.dma_start(out=sp_xln2.ap()[tt], in_=xln2t), "sp", 16)
        barrier(0, [(0, mx_x)], amx)
        rowop(lambda e: e.tensor_scalar(rowbuf[0:1, ROW_SX:ROW_SX + 1], rowbuf[0:1, 0:1],
                                        1.0 / QMAXA, 1e-12, ALU.mult, ALU.max))
        rv = rowop(lambda e: e.reciprocal(cellrow[0:1, CELL_INVSX:CELL_INVSX + 1],
                                          rowbuf[0:1, ROW_SX:ROW_SX + 1]))
        g1 = bcast_cells(CELL_INVSX, CELL_INVSX + 1, rv)
        return g1, spill_t

    def p4(b1_gp, p3_sp):
        Xq = arv(XQ_OFF, BF16, [128, 8, 2048])
        xl2 = [arv(OFF_XL2LD0, F32, [128, 1024]), arv(OFF_XL2LD1, F32, [128, 1024])]
        xqi = arv(OFF_XQI, F32, [128, 1024])
        xqbf = arv(OFF_XQBF, BF16, [128, 1024])
        upT = arv(OFF_WUPT, BF16, [128, 8, 3072])
        gT = arv(OFF_WGT, BF16, [128, 8, 3072])
        stg = [arv(OFF_F1ST0, BF16, [128, 2048]), arv(OFF_F1ST1, BF16, [128, 2048])]

        WAIT("sync", "sp", p3_sp)
        ld_free = [0, 0]
        pm_free = [dq_state["pm_free"][0], dq_state["pm_free"][1]]
        prev_trt = dq_state["last_tm"]
        evt = 0
        for tt in range(NTT):
            b = tt % 2
            WAIT("sync", "vx", ld_free[b])
            lt = TICK("sync", lambda e, tt=tt, b=b: e.dma_start(out=xl2[b], in_=sp_xln2.ap()[tt]), "ld", 16)
            WAIT("vector", "ld", lt)
            WAIT("vector", "ld", b1_gp)
            WAIT("vector", "tm", prev_trt)
            ld_free[b] = TICK("vector", lambda e, b=b: e.tensor_scalar(
                xqi, xl2[b], CL(CELL_INVSX), MAGIC, ALU.mult, ALU.add), "vx")
            vq = TICK("vector", lambda e: e.tensor_scalar(xqbf, xqi, -MAGIC, None, ALU.add), "vx")
            WAIT("tensor", "vx", vq)
            for grp in range(2):
                WAIT("tensor", "vx", pm_free[0])
                trt = 0
                for ic in range(4):
                    fn = (lambda e, i=grp * 4 + ic, j=ic: e.transpose(
                        pmA[:, j * 128:(j + 1) * 128], xqbf[:, i * 128:(i + 1) * 128], ident_sb))
                    trt = TICK("tensor", fn, "tm") if ic == 3 else NOTICK("tensor", fn) or trt
                prev_trt = trt
                WAIT("vector", "tm", trt)
                pm_free[0] = TICK("vector", lambda e, tt=tt, g=grp: e.tensor_copy(
                    Xq[:, g * 4:(g + 1) * 4, tt * 128:(tt + 1) * 128],
                    pmA[:, :].rearrange("p (a b) -> p a b", b=128)), "vx")
            evt = pm_free[0]
        sp_map = {0: sp_pu, 1: sp_pg}
        wmap = {0: upT, 1: gT}
        mmax = {0: mx_u, 1: mx_g}
        slot_free = {}
        stg_free = [0, 0]
        sp_t = 0
        vamx = 0
        WAIT("tensor", "vx", evt)
        WAIT("tensor", "vx", dq_state["last_vx"])
        for wi in range(2):
            for oc in range(NOC):
                si = oc % 2
                for ttt in range(4):
                    WAIT("tensor", "sc", slot_free.get(ttt, 0))
                    mmt = 0
                    for dc in range(8):
                        fn = (lambda e, w=wmap[wi], dc=dc, oc=oc, ttt=ttt:
                              e.matmul(pq[:, ttt * 512:(ttt + 1) * 512],
                                       w[:, dc, oc * 128:(oc + 1) * 128],
                                       Xq[:, dc, ttt * 512:(ttt + 1) * 512],
                                       start=(dc == 0), stop=(dc == 7)))
                        mmt = TICK("tensor", fn, "tm") if dc == 7 else NOTICK("tensor", fn) or mmt
                    WAIT("scalar", "tm", mmt)
                    WAIT("scalar", "sp", stg_free[si])
                    slot_free[ttt] = TICK("scalar", lambda e, ttt=ttt, si=si: e.copy(
                        stg[si][:, ttt * 512:(ttt + 1) * 512], pq[:, ttt * 512:(ttt + 1) * 512]), "sc")
                WAIT("vector", "sc", slot_free[3])
                vsync(lambda e, si=si: e.reduce_max(tmx[:, :], stg[si], axis=AX.X,
                                                    apply_absolute_value=True))
                vamx = TICK("vector", lambda e, wi=wi: e.tensor_tensor(
                    mmax[wi][:, :], mmax[wi][:, :], tmx[:, :], ALU.max), "vx")
                WAIT("gpsimd", "vx", vamx)
                sp_t = TICK("gpsimd", lambda e, wi=wi, oc=oc, si=si: e.dma_start(
                    out=sp_map[wi].ap()[oc], in_=stg[si]), "sp", 16)
                stg_free[si] = sp_t
        barrier(1, [(0, mx_u), (1, mx_g)], vamx)
        rowop(lambda e: e.tensor_tensor(rowbuf[0:1, ROW_TA:ROW_TA + 1], rowbuf[0:1, 0:1],
                                        rowbuf[0:1, ROW_SX:ROW_SX + 1], ALU.mult))
        rowop(lambda e: e.tensor_scalar(rowbuf[0:1, ROW_SU:ROW_SU + 1], rowbuf[0:1, ROW_TA:ROW_TA + 1],
                                        1.0 / QMAXA, 1e-12, ALU.mult, ALU.max))
        rowop(lambda e: e.tensor_tensor(rowbuf[0:1, ROW_TA:ROW_TA + 1], rowbuf[0:1, 1:2],
                                        rowbuf[0:1, ROW_SX:ROW_SX + 1], ALU.mult))
        rowop(lambda e: e.tensor_scalar(rowbuf[0:1, ROW_SG:ROW_SG + 1], rowbuf[0:1, ROW_TA:ROW_TA + 1],
                                        1.0 / QMAXA, 1e-12, ALU.mult, ALU.max))
        rowop(lambda e: e.reciprocal(rowbuf[0:1, ROW_TA:ROW_TA + 1], rowbuf[0:1, ROW_SG:ROW_SG + 1]))
        rowop(lambda e: e.tensor_tensor(cellrow[0:1, CELL_C1:CELL_C1 + 1], rowbuf[0:1, ROW_SX:ROW_SX + 1],
                                        rowbuf[0:1, ROW_TA:ROW_TA + 1], ALU.mult))
        rowop(lambda e: e.tensor_copy(cellrow[0:1, CELL_SG:CELL_SG + 1], rowbuf[0:1, ROW_SG:ROW_SG + 1]))
        rv = rowop(lambda e: e.tensor_scalar(cellrow[0:1, CELL_BSG:CELL_BSG + 1], rowbuf[0:1, ROW_SG:ROW_SG + 1],
                                             -B_ACT, None, ALU.mult))
        g = bcast_cells(CELL_C1, CELL_BSG + 1, rv)
        return g, sp_t

    def p5(b2_gp, p4_sp):
        pgld = [arv(OFF_PGLD0, BF16, [128, 2048]), arv(OFF_PGLD1, BF16, [128, 2048])]
        t1i = arv(OFF_T1I, F32, [128, 2048])
        sig = arv(OFF_SIG, F32, [128, 2048])
        t2i = arv(OFF_T2I, F32, [128, 2048])
        t1f = arv(OFF_T1F, F32, [128, 2048])
        t3 = [arv(OFF_T3A, F32, [128, 2048]), arv(OFF_T3B, F32, [128, 2048])]
        WAIT("sync", "sp", p4_sp)
        ld_free = [0, 0]
        t3_free = [0, 0]
        sp_t = 0
        vamx = 0
        for oc in range(NOC):
            b = oc % 2
            WAIT("sync", "vx", ld_free[b])
            lt = TICK("sync", lambda e, oc=oc, b=b: e.dma_start(out=pgld[b], in_=sp_pg.ap()[oc]), "ld", 16)
            WAIT("vector", "ld", lt)
            WAIT("vector", "ld", b2_gp)
            NOTICK("vector", lambda e, b=b: e.tensor_scalar(
                t1i, pgld[b], CL(CELL_C1), MAGIC, ALU.mult, ALU.add))
            vt1 = TICK("vector", lambda e: e.tensor_scalar(t1f, t1i, -MAGIC, None, ALU.add), "vx")
            WAIT("scalar", "vx", vt1)
            st_ = TICK("scalar", lambda e: e.activation(
                sig, t1f, AF.Sigmoid, scale=CL(CELL_SG)), "sc")
            ld_free[b] = vt1
            WAIT("vector", "sc", st_)
            NOTICK("vector", lambda e: e.tensor_scalar(t2i, sig, 65536.0, MAGIC, ALU.mult, ALU.add))
            NOTICK("vector", lambda e: e.tensor_scalar(t2i, t2i, -MAGIC, 65535.0, ALU.add, ALU.min))
            WAIT("vector", "sp", t3_free[b])
            TICK("vector", lambda e, b=b: e.tensor_tensor(t3[b], t1f, t2i, ALU.mult), "vx")
            vsync(lambda e, b=b: e.reduce_max(tmx[:, :], t3[b], axis=AX.X,
                                                apply_absolute_value=True))
            vamx = TICK("vector", lambda e: e.tensor_tensor(mx_3[:, :], mx_3[:, :], tmx[:, :], ALU.max), "vx")
            WAIT("gpsimd", "vx", vamx)
            sp_t = TICK("gpsimd", lambda e, oc=oc, b=b: e.dma_start(out=sp_t3.ap()[oc], in_=t3[b]), "sp", 16)
            t3_free[b] = sp_t
        barrier(2, [(0, mx_3)], vamx)
        rowop(lambda e: e.tensor_scalar(rowbuf[0:1, ROW_TA:ROW_TA + 1], rowbuf[0:1, ROW_SG:ROW_SG + 1],
                                        1.0 / 65536.0, None, ALU.mult))
        rowop(lambda e: e.tensor_tensor(rowbuf[0:1, ROW_TB:ROW_TB + 1], rowbuf[0:1, 0:1],
                                        rowbuf[0:1, ROW_TA:ROW_TA + 1], ALU.mult))
        rowop(lambda e: e.tensor_scalar(rowbuf[0:1, ROW_S2:ROW_S2 + 1], rowbuf[0:1, ROW_TB:ROW_TB + 1],
                                        1.0 / QMAXA, 1e-12, ALU.mult, ALU.max))
        rowop(lambda e: e.reciprocal(rowbuf[0:1, ROW_TB:ROW_TB + 1], rowbuf[0:1, ROW_S2:ROW_S2 + 1]))
        rowop(lambda e: e.tensor_tensor(cellrow[0:1, CELL_C3:CELL_C3 + 1], rowbuf[0:1, ROW_TA:ROW_TA + 1],
                                        rowbuf[0:1, ROW_TB:ROW_TB + 1], ALU.mult))
        rowop(lambda e: e.reciprocal(rowbuf[0:1, ROW_TA:ROW_TA + 1], rowbuf[0:1, ROW_SU:ROW_SU + 1]))
        rv = rowop(lambda e: e.tensor_tensor(cellrow[0:1, CELL_C4:CELL_C4 + 1], rowbuf[0:1, ROW_SX:ROW_SX + 1],
                                             rowbuf[0:1, ROW_TA:ROW_TA + 1], ALU.mult))
        g = bcast_cells(CELL_C3, CELL_C4 + 1, rv)
        return g, sp_t

    def p6(b3_gp, p5_sp):
        puld = [arv(OFF_PULD0, BF16, [128, 2048]), arv(OFF_PULD1, BF16, [128, 2048])]
        t3ld = [arv(OFF_T3LD0, F32, [128, 2048]), arv(OFF_T3LD1, F32, [128, 2048])]
        t4i = arv(OFF_T4I, F32, [128, 2048])
        t5i = arv(OFF_T5I, F32, [128, 2048])
        t5f = arv(OFF_T5F, F32, [128, 2048])
        t6 = [arv(OFF_T6A, F32, [128, 2048]), arv(OFF_T6B, F32, [128, 2048])]
        WAIT("sync", "sp", p5_sp)
        ld_free = [0, 0]
        t6_free = [0, 0]
        sp_t = 0
        vamx = 0
        for oc in range(NOC):
            b = oc % 2
            WAIT("sync", "vx", ld_free[b])
            TICK("sync", lambda e, oc=oc, b=b: e.dma_start(out=puld[b], in_=sp_pu.ap()[oc]), "ld", 16)
            lt = TICK("sync", lambda e, oc=oc, b=b: e.dma_start(out=t3ld[b], in_=sp_t3.ap()[oc]), "ld", 16)
            WAIT("vector", "ld", lt)
            WAIT("vector", "ld", b3_gp)
            NOTICK("vector", lambda e, b=b: e.tensor_scalar(
                t4i, t3ld[b], CL(CELL_C3), MAGIC, ALU.mult, ALU.add))
            ld_free[b] = TICK("vector", lambda e, b=b: e.tensor_scalar(
                t5i, puld[b], CL(CELL_C4), MAGIC, ALU.mult, ALU.add), "vx")
            WAIT("scalar", "vx", ld_free[b])
            st_ = TICK("scalar", lambda e: e.activation(t5f, t5i, AF.Identity, bias=CL(CELL_NEGMG)), "sc")
            WAIT("vector", "sc", st_)
            WAIT("vector", "sp", t6_free[b])
            TICK("vector", lambda e, b=b: e.scalar_tensor_tensor(
                t6[b], t4i, -MAGIC, t5f, ALU.add, ALU.mult), "vx")
            vsync(lambda e, b=b: e.reduce_max(tmx[:, :], t6[b], axis=AX.X,
                                                apply_absolute_value=True))
            vamx = TICK("vector", lambda e: e.tensor_tensor(mx_6[:, :], mx_6[:, :], tmx[:, :], ALU.max), "vx")
            WAIT("gpsimd", "vx", vamx)
            sp_t = TICK("gpsimd", lambda e, oc=oc, b=b: e.dma_start(out=sp_t6.ap()[oc], in_=t6[b]), "sp", 16)
            t6_free[b] = sp_t
        barrier(3, [(0, mx_6)], vamx)
        rowop(lambda e: e.tensor_tensor(rowbuf[0:1, ROW_TA:ROW_TA + 1], rowbuf[0:1, ROW_S2:ROW_S2 + 1],
                                        rowbuf[0:1, ROW_SU:ROW_SU + 1], ALU.mult))
        rowop(lambda e: e.tensor_tensor(rowbuf[0:1, ROW_TB:ROW_TB + 1], rowbuf[0:1, 0:1],
                                        rowbuf[0:1, ROW_TA:ROW_TA + 1], ALU.mult))
        rowop(lambda e: e.tensor_scalar(rowbuf[0:1, ROW_SO:ROW_SO + 1], rowbuf[0:1, ROW_TB:ROW_TB + 1],
                                        1.0 / QMAXA, 1e-12, ALU.mult, ALU.max))
        rowop(lambda e: e.reciprocal(rowbuf[0:1, ROW_TB:ROW_TB + 1], rowbuf[0:1, ROW_SO:ROW_SO + 1]))
        rowop(lambda e: e.tensor_tensor(cellrow[0:1, CELL_C5:CELL_C5 + 1], rowbuf[0:1, ROW_TA:ROW_TA + 1],
                                        rowbuf[0:1, ROW_TB:ROW_TB + 1], ALU.mult))
        rowop(lambda e: e.tensor_copy(cellrow[0:1, CELL_SO:CELL_SO + 1], rowbuf[0:1, ROW_SO:ROW_SO + 1]))
        rv = rowop(lambda e: e.tensor_tensor(cellrow[0:1, CELL_SO2:CELL_SO2 + 1], rowbuf[0:1, ROW_SO:ROW_SO + 1],
                                             rowbuf[0:1, ROW_SO:ROW_SO + 1], ALU.mult))
        g = bcast_cells(CELL_C5, CELL_SO2 + 1, rv)
        return g, sp_t

    def p7(b4_gp, p6_sp):
        t6ld = [arv(OFF_T6LD0, F32, [128, 2048]), arv(OFF_T6LD1, F32, [128, 2048])]
        t7i = arv(OFF_T7I, F32, [128, 2048])
        t7 = arv(OFF_T7, BF16, [128, NOC, 2048])
        wdT = arv(OFF_WDT, BF16, [128, NOC, 1024])
        xmld = [arv(OFF_XMLD0, F32, [128, 1024]), arv(OFF_XMLD1, F32, [128, 1024])]
        outb = [arv(OFF_OUTB0, F32, [128, 1024]), arv(OFF_OUTB1, F32, [128, 1024])]

        WAIT("sync", "sp", p6_sp)
        wd_ld = 0
        for ic2 in range(0, NOC, 4):
            wd_ld = TICK("sync", lambda e, ic2=ic2: e.dma_start(
                out=wdT[:, ic2:ic2 + 4, :],
                in_=sp_wd.ap()[ic2:ic2 + 4].transpose([1, 0, 2])), "ld", 16)
        ld_free = [0, 0]
        t7t = 0
        for oc in range(NOC):
            b = oc % 2
            WAIT("sync", "vx", ld_free[b])
            lt = TICK("sync", lambda e, oc=oc, b=b: e.dma_start(out=t6ld[b], in_=sp_t6.ap()[oc]), "ld", 16)
            WAIT("vector", "ld", lt)
            WAIT("vector", "ld", b4_gp)
            ld_free[b] = TICK("vector", lambda e, b=b: e.tensor_scalar(
                t7i, t6ld[b], CL(CELL_C5), MAGIC, ALU.mult, ALU.add), "vx")
            t7t = TICK("vector", lambda e, oc=oc: e.tensor_scalar(
                t7[:, oc, :], t7i, -MAGIC, None, ALU.add), "vx")
        WAIT("tensor", "vx", t7t)
        WAIT("tensor", "ld", wd_ld)
        xm_free = [0, 0]
        mm_free = [("vx", 0), ("vx", 0)]
        out_free = [0, 0]
        store_t = 0
        prev_dsb_rd = 0
        for tt in range(NTT):
            pb = tt % 2
            psv = pst if pb == 0 else pq
            WAIT("sync", "vx", xm_free[pb])
            xmt = TICK("sync", lambda e, tt=tt, b=pb: e.dma_start(out=xmld[b], in_=sp_xmid.ap()[tt]), "ld", 16)
            WAIT("tensor", mm_free[pb][0], mm_free[pb][1])
            mmt = 0
            for oh in range(2):
                for ic in range(NOC):
                    fn = (lambda e, ps=psv, oh=oh, ic=ic, tt=tt:
                          e.matmul(ps[:, oh * 512:(oh + 1) * 512],
                                   t7[:, ic, tt * 128:(tt + 1) * 128],
                                   wdT[:, ic, oh * 512:(oh + 1) * 512],
                                   start=(ic == 0), stop=(ic == NOC - 1)))
                    mmt = TICK("tensor", fn, "tm") if (ic == NOC - 1 and oh == 1) else NOTICK("tensor", fn) or mmt
            WAIT("vector", "tm", mmt)
            dsb = arv(OFF_T7I, F32, [128, 1024])
            WAIT("vector", "sc", prev_dsb_rd)
            vt0 = TICK("vector", lambda e, ps=psv: e.tensor_copy(dsb, ps[:, 0:1024]), "vx")
            NOTICK("vector", lambda e: e.bn_stats(bn6[:, 0:6], dsb[:, 0:512]))
            vsync(lambda e: e.bn_stats(bn6[:, 6:12], dsb[:, 512:1024]))
            vsync(lambda e: e.bn_aggr(bn2[:, :], bn6[:, :]))
            vt = TICK("vector", lambda e: e.tensor_scalar(
                r1[:, :], bn2[:, 1:2], CL(CELL_SO2), EPS, ALU.mult, ALU.add), "vx")
            WAIT("scalar", "vx", vt)
            NOTICK("scalar", lambda e: e.activation(r1[:, :], r1[:, :], AF.Ln))
            st_ = TICK("scalar", lambda e: e.activation(r1[:, :], r1[:, :], AF.Exp, scale=CL(CELL_MHALF)), "sc")
            WAIT("vector", "sc", st_)
            vsync(lambda e: e.tensor_scalar(r1[:, :], r1[:, :], CL(CELL_SO), None, ALU.mult))
            vt = TICK("vector", lambda e: e.tensor_scalar(
                nmrs[:, :], bn2[:, 0:1], r1[:, :], -1.0, ALU.mult, ALU.mult), "vx")
            WAIT("scalar", "vx", vt)
            WAIT("scalar", "st", out_free[pb])
            dsb = arv(OFF_T7I, F32, [128, 1024])
            st_ = TICK("scalar", lambda e, pb=pb: e.activation(
                outb[pb], dsb, AF.Identity, bias=nmrs[:, :], scale=r1[:, :]), "sc")
            prev_dsb_rd = st_
            mm_free[pb] = ("vx", vt0)
            WAIT("vector", "sc", st_)
            WAIT("vector", "ld", xmt)
            xm_free[pb] = TICK("vector", lambda e, pb=pb, b=pb: e.tensor_tensor(
                outb[pb], outb[pb], xmld[b], ALU.add), "vx")
            WAIT("sync", "vx", xm_free[pb])
            store_t = TICK("sync", lambda e, tt=tt, pb=pb: e.dma_start(
                out=out.ap()[tt * 128:(tt + 1) * 128, :], in_=outb[pb]), "st", 16)
            out_free[pb] = store_t
        WAIT("sync", "st", store_t)
        return store_t

    import os as _os
    _plim = int(_os.environ.get("KERNEL_PHASE_LIMIT", "9"))
    p0()
    final_t = 0
    if _plim >= 1:
        qkv_sc, qkv_vx, qkv_tm = p1()
        if _DBG_QKV[0]:
            WAIT("sync", qkv_sc[0], qkv_sc[1])
            TICK("sync", lambda e: e.dma_start(
                out=dbg_q0.ap(), in_=arv(off_Q(0), BF16, [128, 4096])), "st", 16)
            TICK("sync", lambda e: e.dma_start(
                out=dbg_k0.ap(), in_=arv(off_K(0), BF16, [128, 4096])), "st", 16)
            TICK("sync", lambda e: e.dma_start(
                out=dbg_v0.ap(), in_=arv(off_V(0), BF16, [128, 4160])), "st", 16)
            TICK("sync", lambda e: e.dma_start(
                out=dbg_pqv.ap(), in_=arv(62592, F32, [128, 2048])), "st", 16)
            TICK("sync", lambda e: e.dma_start(
                out=dbg_xf.ap(), in_=arv(OFF_XF, BF16, [128, 4096])), "st", 16)
            dqt = TICK("sync", lambda e: e.dma_start(
                out=dbg_wv.ap(), in_=arv(OFF_WVBF, BF16, [128, 4096])), "st", 16)
            WAIT("sync", "st", dqt)
    if _plim >= 2:
        cc_ticks = p2(qkv_sc, qkv_vx, qkv_tm)
    if _plim >= 3:
        b1_gp, p3_sp = p3(cc_ticks)
    if _plim >= 4:
        b2_gp, p4_sp = p4(b1_gp, p3_sp)
    if _plim >= 5:
        b3_gp, p5_sp = p5(b2_gp, p4_sp)
    if _plim >= 6:
        b4_gp, p6_sp = p6(b3_gp, p5_sp)
    if _plim >= 7:
        final_t = p7(b4_gp, p6_sp)

    if _DBG_CELLS[0]:
        WAIT("sync", "st", final_t)
        TICK("sync", lambda e: e.dma_start(out=dbg_cells.ap(), in_=cells[:, :]), "st", 16)
        TICK("sync", lambda e: e.dma_start(out=dbg_row.ap()[0:1, :], in_=rowbuf[:, :]), "st", 16)
        TICK("sync", lambda e: e.dma_start(out=dbg_row.ap()[1:2, :], in_=cellrow[:, :]), "st", 16)
        TICK("sync", lambda e: e.dma_start(out=dbg_small.ap()[:, 0:1], in_=r1[:, :],
                                           allow_slow_non_contiguous=True), "st", 16)
        TICK("sync", lambda e: e.dma_start(out=dbg_small.ap()[:, 1:2], in_=nmrs[:, :],
                                           allow_slow_non_contiguous=True), "st", 16)
        TICK("sync", lambda e: e.dma_start(out=dbg_small.ap()[:, 2:4], in_=bn2[:, :],
                                           allow_slow_non_contiguous=True), "st", 16)
        dsbv = arv(OFF_T7I, F32, [128, 1024])
        obv = arv(OFF_OUTB0, F32, [128, 1024])
        TICK("sync", lambda e: e.dma_start(out=dbg_dsb.ap()[:, 0:1024], in_=dsbv), "st", 16)
        final_t = TICK("sync", lambda e: e.dma_start(out=dbg_dsb.ap()[:, 1024:2048], in_=obv), "st", 16)
    for name, (dst, srch) in ddbg.items():
        WAIT("sync", "st", final_t)
        final_t = TICK("sync", lambda e, d=dst, s=srch: e.dma_start(out=d.ap(), in_=s.ap()), "st", 16)
    WAIT("sync", "st", final_t)
    for (name, off, dt, nel) in taps:
        src = arv(off, dt, [128, nel])
        tmpd = nc.alloc_sbuf_tensor("dbgtmp_" + name, [128, nel], F32)
        WAIT("vector", "st", final_t)
        vt = TICK("vector", lambda e, a=tmpd, b=src: e.tensor_copy(a[:, :], b), "vx")
        WAIT("sync", "vx", vt)
        final_t = TICK("sync", lambda e, nm=name, a=tmpd: e.dma_start(out=dbg[nm].ap(), in_=a[:, :]), "st", 16)
    WAIT("sync", "st", final_t)

    block = es.enter_context(nc.Block())

    import os as _os2
    _otrunc = int(_os2.environ.get("KERNEL_OPS_TRUNC", "0")) or None

    def _mk(engine_name):
        def run(eng):
            for sq, f in ops[engine_name]:
                if _otrunc is not None and sq > _otrunc:
                    continue
                f(eng)
        return run

    block.sync(_mk("sync"))
    block.tensor(_mk("tensor"))
    block.vector(_mk("vector"))
    block.scalar(_mk("scalar"))
    block.gpsimd(_mk("gpsimd"))
    es.close()
    return nc


def _host_masks():
    m = np.zeros((128, 4, 1024), dtype=ml_dtypes.bfloat16)
    for off_i, off in enumerate((0, 128, 256, 384)):
        i = np.arange(128)[:, None]
        j = np.arange(512)[None, :]
        valid = (j >= i + off).astype(ml_dtypes.bfloat16)
        m[:, off_i, 0:512] = valid
        m[:, off_i, 512:1024] = valid
    return np.ascontiguousarray(m.reshape(128, 4096))


def kernel(L0_attn_x, L0_attn_Wqkv, L0_attn_Wout, L0_attn_bout,
           L0_ffn_w_gate, L0_ffn_w_up, L0_ffn_w_down):
    x = np.ascontiguousarray(np.asarray(L0_attn_x, dtype=np.float32))
    Wqkv = np.asarray(L0_attn_Wqkv, dtype=np.float32)
    Wout = np.asarray(L0_attn_Wout, dtype=np.float32)
    bout = np.ascontiguousarray(np.asarray(L0_attn_bout, dtype=np.float32))
    wgh = np.ascontiguousarray(np.asarray(L0_ffn_w_gate, dtype=np.float32))
    wuh = np.ascontiguousarray(np.asarray(L0_ffn_w_up, dtype=np.float32))
    wdh = np.ascontiguousarray(np.asarray(L0_ffn_w_down, dtype=np.float32))
    wout_th = np.ascontiguousarray(Wout.T)
    cmask = _host_masks()
    ident = np.ascontiguousarray(np.eye(128, dtype=ml_dtypes.bfloat16))
    ex1 = np.repeat(np.eye(32, dtype=np.float32), 32, axis=1)
    expander = np.ascontiguousarray(np.concatenate([ex1, ex1], axis=0))

    nc = build_nc(taps=tuple(_DBG_TAPS))
    in_maps = []
    for c in range(NCORE):
        b, h2 = c // 2, c % 2
        qs = np.ascontiguousarray(Wqkv[512 * h2:512 * h2 + 512, :].T)
        ks = np.ascontiguousarray(Wqkv[D + 512 * h2:D + 512 * h2 + 512, :].T)
        vs = np.ascontiguousarray(Wqkv[2 * D + 512 * h2:2 * D + 512 * h2 + 512, :].T)
        in_maps.append({
            "xb": np.ascontiguousarray(x[b]),
            "xhalf": np.ascontiguousarray(x[b, TPC * h2:TPC * h2 + TPC, :]),
            "wq_t": qs, "wk_t": ks, "wv_t": vs,
            "wout_t": wout_th, "boutv": bout,
            "wg": wgh, "wu": wuh, "wd": wdh,
            "cmask2": cmask, "identb": ident,
            "hselin": np.ascontiguousarray(
                np.broadcast_to(np.array([1.0 - h2, float(h2)], np.float32), (128, 2))),
            "exin": expander,
        })
    res = run_bass_kernel_spmd(nc, in_maps, core_ids=list(range(NCORE)))
    _RES_HOOK[0] = res
    outp = np.empty((B, T, D), dtype=np.float32)
    for c in range(NCORE):
        b, h2 = c // 2, c % 2
        outp[b, TPC * h2:TPC * h2 + TPC, :] = res.results[c]["out"]
    return outp


# version 15
# speedup vs baseline: 1.0019x; 1.0019x over previous
import numpy as np
import ml_dtypes
import concourse.bass as bass
import concourse.mybir as mybir
from concourse.bass_utils import run_bass_kernel_spmd

F32 = mybir.dt.float32
BF16 = mybir.dt.bfloat16
I32 = mybir.dt.int32
AF = mybir.ActivationFunctionType
ALU = mybir.AluOpType
AX = mybir.AxisListType

B, T, D, H = 4, 4096, 1024, 16
NCORE = 8
TPC = 2048
DFF = 3072
NOC = DFF // 128
NDC = D // 128
NTT = TPC // 128
EPS = 1e-5
B_ACT = 32768.0
MAGIC = 12582912.0
B_W = 24.0
EXPC = 12.0
QMAXA = 32767.0
QMAXW = 7.0

_DBG_TAPS = []
_DBG_DRAM = []
_DBG_CELLS = [False]
_DBG_QKV = [False]
_DBG_SEQ = None
_RES_HOOK = [None]

AR_BYTES = 196608
OFF_CMASK = 0
OFF_IDENT = 8192
OFF_BOUT = 8448
OFF_CELLS = 12544
OFF_A = 13312
OFF_QKV = 37888
HP_STRIDE = 24704
def off_Q(hp): return OFF_QKV + hp * HP_STRIDE
def off_K(hp): return OFF_QKV + hp * HP_STRIDE + 8192
def off_V(hp): return OFF_QKV + hp * HP_STRIDE + 16384
OFF_O = 136704
OFF_PT0 = 169472
OFF_PT1 = 171520
OFF_RB64 = 173568
OFF_DENLN = 177664
OFF_OBT0 = 181760
OFF_OBT1 = 183808
OFF_SA = 185856
OFF_WDNAT = 189952
OFF_TAIL2 = 192000
OFF_REP2 = 196096
OFF_XST0 = OFF_O
OFF_XST1 = OFF_O + 16384
OFF_WQST = OFF_O
OFF_XF = 169472
OFF_XLN0 = 177664
OFF_XLN1 = 179712
OFF_WQBF = 13312
OFF_WKBF = 21504
OFF_WVBF = 29696
OFF_WUPT = 62592
OFF_WGT = 111744
AFULL_OFFS = [160896, 164992, 169472, 173568, 177664, 181760, 185856, 189952]
OFF_WOUTBF = 13312
OFF_YSB = 29696
OFF_XMIDT = 33792
OFF_XLN2T = 37888
OFF_XH0 = 41984
OFF_XH1 = 46080
XQ_OFF = 13312
OFF_XL2LD0 = 46080
OFF_XL2LD1 = 50176
OFF_XQI = 54272
OFF_XQBF = 58368
OFF_F1ST0 = 160896
OFF_F1ST1 = 164992
OFF_PGLD0 = 46080
OFF_PGLD1 = 50176
OFF_T1I = 54272
OFF_SIG = 13312
OFF_T2I = 21504
OFF_T1F = 29696
OFF_T3A = 160896
OFF_T3B = 169472
OFF_PULD0 = 46080
OFF_PULD1 = 50176
OFF_T3LD0 = 13312
OFF_T3LD1 = 21504
OFF_T4I = 29696
OFF_T5I = 54272
OFF_T5F = 37888
OFF_T6A = 160896
OFF_T6B = 169472
OFF_WDT = 13312
OFF_T7 = 62592
OFF_T6LD0 = 160896
OFF_T6LD1 = 169472
OFF_T7I = 177664
OFF_XMLD0 = 185856
OFF_XMLD1 = 189952
OFF_OUTB0 = 160896
OFF_OUTB1 = 164992

CELL_SCALE8, CELL_NEGC, CELL_EPS, CELL_INVSX, CELL_C1, CELL_SG, CELL_BSG, \
    CELL_C3, CELL_C4, CELL_C5, CELL_SO, CELL_SO2, CELL_ZERO, CELL_ONE, \
    CELL_NEGBA, CELL_MHALF = range(16)
CELL_NEGMG = 16

ROW_SX, ROW_SG, ROW_SU, ROW_S2, ROW_SO, ROW_TA, ROW_TB, ROW_P0, ROW_P1, ROW_P2 = \
    8, 9, 10, 11, 12, 13, 14, 20, 21, 22


def _dt_size(dt):
    return {F32: 4, BF16: 2, I32: 4}[dt]


def build_nc(taps=()):
    nc = bass.Bass("TRN2", target_bir_lowering=False, detect_race_conditions=False)

    xb = nc.dram_tensor("xb", [T, D], F32, kind="ExternalInput")
    xhalf = nc.dram_tensor("xhalf", [TPC, D], F32, kind="ExternalInput")
    wq_t = nc.dram_tensor("wq_t", [D, 512], F32, kind="ExternalInput")
    wk_t = nc.dram_tensor("wk_t", [D, 512], F32, kind="ExternalInput")
    wv_t = nc.dram_tensor("wv_t", [D, 512], F32, kind="ExternalInput")
    wout_t = nc.dram_tensor("wout_t", [D, D], F32, kind="ExternalInput")
    boutv = nc.dram_tensor("boutv", [D], F32, kind="ExternalInput")
    wg = nc.dram_tensor("wg", [DFF, D], F32, kind="ExternalInput")
    wu = nc.dram_tensor("wu", [DFF, D], F32, kind="ExternalInput")
    wd = nc.dram_tensor("wd", [D, DFF], F32, kind="ExternalInput")
    cmask2 = nc.dram_tensor("cmask2", [128, 4096], BF16, kind="ExternalInput")
    identb = nc.dram_tensor("identb", [128, 128], BF16, kind="ExternalInput")
    hselin = nc.dram_tensor("hselin", [128, 2], F32, kind="ExternalInput")
    exin = nc.dram_tensor("exin", [64, 1024], F32, kind="ExternalInput")
    out = nc.dram_tensor("out", [TPC, D], F32, kind="ExternalOutput")

    sp_pu = nc.dram_tensor("sp_pu", [NOC, 128, TPC], BF16)
    sp_pg = nc.dram_tensor("sp_pg", [NOC, 128, TPC], BF16)
    sp_t3 = nc.dram_tensor("sp_t3", [NOC, 128, TPC], F32)
    sp_t6 = nc.dram_tensor("sp_t6", [NOC, 128, TPC], F32)
    sp_xmid = nc.dram_tensor("sp_xmid", [NTT, 128, D], F32)
    sp_xln2 = nc.dram_tensor("sp_xln2", [NTT, 128, D], F32)
    sp_wd = nc.dram_tensor("sp_wd", [NOC, 128, D], BF16)
    cc_ex_in = [nc.dram_tensor(f"cc_exi{hp}", [128, T], BF16) for hp in range(4)]
    cc_ex_out = [nc.dram_tensor(f"cc_exo{hp}", [2, 128, T], BF16) for hp in range(4)]
    cc_b_in = [nc.dram_tensor(f"cc_bi{i}", [1, 8], F32) for i in range(4)]
    cc_b_out = [nc.dram_tensor(f"cc_bo{i}", [8, 8], F32) for i in range(4)]
    rb_d = nc.dram_tensor("rb_d", [1, 2048], F32)
    cdram = nc.dram_tensor("cdram", [1, 64], F32)
    dbg = {}
    for (name, off, dt, nel) in taps:
        dbg[name] = nc.dram_tensor("dbg_" + name, [128, nel], F32, kind="ExternalOutput")
    dram_map = {"sp_pu": (sp_pu, [NOC, 128, TPC], BF16), "sp_pg": (sp_pg, [NOC, 128, TPC], BF16),
                "sp_t3": (sp_t3, [NOC, 128, TPC], F32), "sp_t6": (sp_t6, [NOC, 128, TPC], F32),
                "sp_xmid": (sp_xmid, [NTT, 128, D], F32), "sp_xln2": (sp_xln2, [NTT, 128, D], F32),
                "sp_wd": (sp_wd, [NOC, 128, D], BF16),
                "cc_exo0": (cc_ex_out[0], [2, 128, T], BF16),
                "cc_bo0": (cc_b_out[0], [8, 8], F32), "cc_bo1": (cc_b_out[1], [8, 8], F32),
                "cc_bo2": (cc_b_out[2], [8, 8], F32), "cc_bo3": (cc_b_out[3], [8, 8], F32)}
    if _DBG_QKV[0]:
        dbg_q0 = nc.dram_tensor("dbg_q0", [128, 4096], BF16, kind="ExternalOutput")
        dbg_k0 = nc.dram_tensor("dbg_k0", [128, 4096], BF16, kind="ExternalOutput")
        dbg_v0 = nc.dram_tensor("dbg_v0", [128, 4160], BF16, kind="ExternalOutput")
        dbg_pqv = nc.dram_tensor("dbg_pqv", [128, 2048], F32, kind="ExternalOutput")
        dbg_xf = nc.dram_tensor("dbg_xf", [128, 4096], BF16, kind="ExternalOutput")
        dbg_wv = nc.dram_tensor("dbg_wv", [128, 4096], BF16, kind="ExternalOutput")
    if _DBG_CELLS[0]:
        dbg_cells = nc.dram_tensor("dbg_cells", [128, 64], F32, kind="ExternalOutput")
        dbg_row = nc.dram_tensor("dbg_row", [2, 64], F32, kind="ExternalOutput")
        dbg_small = nc.dram_tensor("dbg_small", [128, 8], F32, kind="ExternalOutput")
        dbg_dsb = nc.dram_tensor("dbg_dsb", [128, 2048], F32, kind="ExternalOutput")
    ddbg = {}
    for name in _DBG_DRAM:
        h, shp, dt = dram_map[name]
        ddbg[name] = (nc.dram_tensor("ddbg_" + name, shp, dt, kind="ExternalOutput"), h)

    AR = nc.alloc_sbuf_tensor("arena", [128, AR_BYTES], mybir.dt.uint8)

    def arv(off, dt, shape, p0=0):
        parts = shape[0]
        n = 1
        for s in shape[1:]:
            n *= s
        ap = AR[p0:p0 + parts, off:off + n * _dt_size(dt)].bitcast(dt)
        if len(shape) == 3:
            ap = ap.rearrange("p (a b) -> p a b", b=shape[2])
        elif len(shape) == 4:
            ap = ap.rearrange("p (a b c) -> p a b c", b=shape[2], c=shape[3])
        return ap

    sbt = nc.alloc_sbuf_tensor
    rowbuf = sbt("rowbuf", [1, 64], F32)
    rowcc = sbt("rowcc", [1, 64], F32)
    bn6 = sbt("bn6", [128, 12], F32)
    bn2 = sbt("bn2", [128, 2], F32)
    r1 = sbt("r1", [128, 1], F32)
    r1v = sbt("r1v", [128, 1], F32)
    nmrs = sbt("nmrs", [128, 1], F32)
    tmx = sbt("tmx", [128, 1], F32)
    par = sbt("par", [128, 1], F32)
    mx_x = sbt("mx_x", [128, 1], F32)
    mx_u = sbt("mx_u", [128, 1], F32)
    mx_g = sbt("mx_g", [128, 1], F32)
    mx_3 = sbt("mx_3", [128, 1], F32)
    mx_6 = sbt("mx_6", [128, 1], F32)
    wbmax = sbt("wbmax", [128, 32], F32)
    invpack = sbt("invpack", [128, 64], F32)
    expnd = sbt("expnd", [64, 1024], F32)
    sT_sb = sbt("sT_sb", [64, 128], F32)
    wbinv = invpack[:, 0:32]
    wbs = invpack[:, 32:64]
    hsel = sbt("hsel", [128, 2], F32)
    cellrow = sbt("cellrow", [1, 64], F32)
    identf = sbt("identf", [128, 128], F32)
    zero_cell = sbt("zero_cell", [128, 1], F32)
    one_cell = sbt("one_cell", [128, 1], F32)
    zero_cb = sbt("zero_cb", [128, 1], BF16)
    one_cb = sbt("one_cb", [128, 1], BF16)

    cells = sbt("cells", [128, 64], F32)
    cmask_sb = arv(OFF_CMASK, BF16, [128, 4, 1024])
    ident_sb = arv(OFF_IDENT, BF16, [128, 128])
    bout_rep = arv(OFF_BOUT, F32, [128, 1024])
    boutrow = arv(OFF_BOUT, F32, [1, 1024])

    pq = nc.alloc_psum_tensor("pq", [128, 2048], F32)
    pst = nc.alloc_psum_tensor("pst", [128, 1024], F32)
    pmA = nc.alloc_psum_tensor("pmA", [128, 512], BF16)
    pmR = nc.alloc_psum_tensor("pmR", [128, 512], F32)

    ops = {e: [] for e in ("sync", "tensor", "vector", "scalar", "gpsimd")}
    cnt = {}
    sems = {}
    waited = {e: {} for e in ops}
    from contextlib import ExitStack
    es = ExitStack()
    for name in ("ld", "l2", "sp", "tm", "vx", "sc", "gp", "cc", "st"):
        sems[name] = es.enter_context(nc.semaphore(name))
        cnt[name] = 0

    seqc = [0]

    import traceback as _tb

    def _app(eng, fn):
        seqc[0] += 1
        if _DBG_SEQ and _DBG_SEQ[0] <= seqc[0] <= _DBG_SEQ[1]:
            fr = _tb.extract_stack()[-3]
            print(f"seq {seqc[0]} [{eng}] {fr.name}:{fr.lineno} {fr.line[:90] if fr.line else ''}")
        ops[eng].append((seqc[0], fn))

    def NOTICK(eng, fn):
        _app(eng, fn)

    def TICK(eng, fn, sem, k=1):
        cnt[sem] += k
        h = sems[sem]
        _app(eng, lambda e, fn=fn, h=h, k=k: fn(e).then_inc(h, k))
        return cnt[sem]

    def WAIT(eng, sem, v):
        if not v or v <= 0:
            return
        if waited[eng].get(sem, 0) >= v:
            return
        waited[eng][sem] = v
        h = sems[sem]
        _app(eng, lambda e, h=h, v=v: e.wait_ge(h, v))

    nc.const_aps.register(F32, 0.0, zero_cell[:, :])
    nc.const_aps.register(F32, 1.0, one_cell[:, :])
    nc.const_aps.register(BF16, 0.0, zero_cb[:, :])
    nc.const_aps.register(BF16, 1.0, one_cb[:, :])

    CL = lambda i: cells[:, i:i + 1]

    def ln_rstd(var_ap):
        NOTICK("scalar", lambda e: e.activation(r1[:, :], var_ap, AF.Ln, bias=CL(CELL_EPS)))
        return TICK("scalar", lambda e: e.activation(r1[:, :], r1[:, :], AF.Exp, scale=CL(CELL_MHALF)), "sc")

    def ln_tail(var_ap, mean_ap):
        st_ = ln_rstd(var_ap)
        WAIT("vector", "sc", st_)
        NOTICK("vector", lambda e: e.tensor_copy(r1v[:, :], r1[:, :]))
        vt2 = TICK("vector", lambda e, m=mean_ap: e.scalar_tensor_tensor(
            nmrs[:, :], m, -1.0, r1[:, :], ALU.mult, ALU.mult), "vx")
        WAIT("scalar", "vx", vt2)
        return vt2

    def p0():
        for cell, val in ((zero_cell, 0.0), (one_cell, 1.0)):
            NOTICK("vector", lambda e, c=cell, v=val: e.memset(c[:, :], v))
        for cell, val in ((zero_cb, 0.0), (one_cb, 1.0)):
            NOTICK("vector", lambda e, c=cell, v=val: e.memset(c[:, :], v))
        NOTICK("vector", lambda e: e.memset(cells[:, :], 0.0))
        for idx, val in ((CELL_SCALE8, 0.125), (CELL_NEGC, -EXPC), (CELL_EPS, EPS),
                         (CELL_ONE, 1.0), (CELL_NEGBA, -B_ACT), (CELL_MHALF, -0.5),
                         (CELL_NEGMG, -MAGIC)):
            NOTICK("vector", lambda e, i=idx, v=val: e.memset(CL(i), v))
        for m in (mx_x, mx_u, mx_g, mx_3, mx_6):
            NOTICK("vector", lambda e, m=m: e.memset(m[:, :], 0.0))
        for hp in range(4):
            vV = arv(off_V(hp), BF16, [128, 4160])
            NOTICK("vector", lambda e, a=vV: e.memset(a, 1.0))
        v0 = TICK("vector", lambda e: e.memset(rowbuf[:, :], 0.0), "vx")

        TICK("sync", lambda e: e.dma_start(out=cmask_sb, in_=cmask2.ap().rearrange("p (a b) -> p a b", b=1024)), "ld", 16)
        ti = TICK("sync", lambda e: e.dma_start(out=ident_sb, in_=identb.ap()), "ld", 16)
        TICK("sync", lambda e: e.dma_start(
            out=bout_rep, in_=boutv.ap().unsqueeze(0).partition_broadcast(128).opt()), "ld", 16)
        t4 = TICK("sync", lambda e: e.dma_start(out=hsel[:, :], in_=hselin.ap()), "ld", 16)
        TICK("sync", lambda e: e.dma_start(out=expnd[:, :], in_=exin.ap()), "ld", 16)
        WAIT("vector", "ld", ti)
        TICK("vector", lambda e: e.tensor_copy(identf[:, :], ident_sb), "vx")
        return t4, v0

    def p1():
        stage_offs = (OFF_WQST, OFF_WQST + 16384, 62592)
        cast_ts = []
        lts = []
        for (dram, _), soff in zip(((wq_t, 0), (wk_t, 0), (wv_t, 0)), stage_offs):
            st3 = arv(soff, F32, [128, 8, 512])
            lts.append(TICK("sync", lambda e, d=dram, o=st3: e.dma_start(
                out=o, in_=d.ap().rearrange("(c p) o -> p c o", p=128)), "ld", 16))
        for (dram, woff), soff, lt in zip(
                ((wq_t, OFF_WQBF), (wk_t, OFF_WKBF), (wv_t, OFF_WVBF)), stage_offs, lts):
            WAIT("vector", "ld", lt)
            wbf = arv(woff, BF16, [128, 4096])
            src = arv(soff, F32, [128, 4096])
            cast_ts.append(TICK("vector", lambda e, a=wbf, s=src: e.tensor_copy(a, s), "vx"))
        cast_q, cast_k, cast_t = cast_ts
        wqbf = arv(OFF_WQBF, BF16, [128, 8, 512])
        wkbf = arv(OFF_WKBF, BF16, [128, 8, 512])
        wvbf = arv(OFF_WVBF, BF16, [128, 8, 512])

        xst = [arv(OFF_XST0, F32, [128, 4, 1024]), arv(OFF_XST1, F32, [128, 4, 1024])]
        xln = [arv(OFF_XLN0, BF16, [128, 1024]), arv(OFF_XLN1, BF16, [128, 1024])]
        xf = arv(OFF_XF, BF16, [128, 8, 512])

        consumed = [("vx", cast_q), ("vx", cast_k)]
        pm_ev = [0, 0]
        slot_ev = {}
        prev_tile_mm = 0
        trt = 0
        for tt in range(8):
            buf = tt % 2
            WAIT("sync", consumed[buf][0], consumed[buf][1])
            lt = TICK("sync", lambda e, tt=tt, b=buf: e.dma_start(
                out=xst[b],
                in_=xb.ap()[tt * 512:(tt + 1) * 512, :].rearrange("(a p) d -> p a d", p=128)),
                "ld", 16)
            xf_ev = 0
            for a in range(4):
                xa = xst[buf][:, a, :]
                WAIT("vector", "ld", lt)
                NOTICK("vector", lambda e, x=xa: e.bn_stats(bn6[:, 0:6], x[:, 0:512]))
                vsync(lambda e, x=xa: e.bn_stats(bn6[:, 6:12], x[:, 512:1024]))
                vt = vsync(lambda e: e.bn_aggr(bn2[:, :], bn6[:, :]))
                WAIT("scalar", "vx", vt)
                ln_tail(bn2[:, 1:2], bn2[:, 0:1])
                lnt = TICK("scalar", lambda e, x=xa, o=xln[a % 2]: e.activation(
                    o, x, AF.Identity, bias=nmrs[:, :], scale=r1v[:, :]), "sc")
                if a == 3:
                    consumed[buf] = ("sc", lnt)
                WAIT("tensor", "sc", lnt)
                for grp in range(2):
                    WAIT("tensor", "vx", pm_ev[0])
                    for ic in range(4):
                        fn = (lambda e, i=grp * 4 + ic, j=ic, o=xln[a % 2]:
                              e.transpose(pmA[:, j * 128:(j + 1) * 128], o[:, i * 128:(i + 1) * 128], ident_sb))
                        trt = TICK("tensor", fn, "tm") if ic == 3 else NOTICK("tensor", fn) or trt
                    WAIT("vector", "tm", trt)
                    if a == 0 and grp == 0:
                        WAIT("vector", "tm", prev_tile_mm)
                    pm_ev[0] = TICK("vector", lambda e, a_=a, g=grp: e.tensor_copy(
                        xf[:, g * 4:(g + 1) * 4, a_ * 128:(a_ + 1) * 128],
                        pmA[:, :].rearrange("p (a b) -> p a b", b=128)), "vx")
                xf_ev = pm_ev[0]
            WAIT("tensor", "vx", xf_ev)
            for kind in ("q", "k"):
                wbf_ = wqbf if kind == "q" else wkbf
                for oc in range(4):
                    sem_o, val_o = slot_ev.get(oc, ("sc", 0))
                    WAIT("tensor", sem_o, val_o)
                    for dc in range(NDC):
                        fn = (lambda e, w=wbf_, dc=dc, oc=oc:
                              e.matmul(pq[:, oc * 512:(oc + 1) * 512], w[:, dc, oc * 128:(oc + 1) * 128],
                                       xf[:, dc, :], start=(dc == 0), stop=(dc == NDC - 1)))
                        mmt = TICK("tensor", fn, "tm") if dc == NDC - 1 else NOTICK("tensor", fn) or 0
                    WAIT("scalar", "tm", mmt)
                    dst = arv(off_Q(oc) if kind == "q" else off_K(oc), BF16, [128, 4096])
                    slot_ev[oc] = ("sc", TICK("scalar", lambda e, d=dst, oc=oc, tt=tt: e.copy(
                        d[:, tt * 512:(tt + 1) * 512], pq[:, oc * 512:(oc + 1) * 512]), "sc"))
                    prev_tile_mm = mmt
            for a in range(4):
                sem_a, val_a = slot_ev.get(a, ("sc", 0))
                WAIT("tensor", sem_a, val_a)
                for dc in range(NDC):
                    fn = (lambda e, dc=dc, a=a:
                          e.matmul(pq[:, a * 512:(a + 1) * 512], xf[:, dc, a * 128:(a + 1) * 128],
                                   wvbf[:, dc, :], start=(dc == 0), stop=(dc == NDC - 1)))
                    mmt = TICK("tensor", fn, "tm") if dc == NDC - 1 else NOTICK("tensor", fn) or 0
                WAIT("vector", "tm", mmt)
                kb = tt * 4 + a
                last_v = 0
                for hp in range(4):
                    fn = (lambda e, hp=hp, a=a, kb=kb: e.tensor_copy(
                        arv(off_V(hp), BF16, [128, 2, 32, 65])[:, :, kb, 0:64],
                        pq[:, a * 512:(a + 1) * 512].rearrange("p (h d) -> p h d", d=64)[:, 2 * hp:2 * hp + 2, :]))
                    if hp == 3:
                        last_v = TICK("vector", fn, "vx")
                        slot_ev[a] = ("vx", last_v)
                    else:
                        NOTICK("vector", fn)
                if _DBG_QKV[0] and tt == 7:
                    stv = arv(62592, F32, [128, 2048])
                    last_v = TICK("vector", lambda e, a=a: e.tensor_copy(
                        stv[:, a * 512:(a + 1) * 512], pq[:, a * 512:(a + 1) * 512]), "vx")
                    slot_ev[a] = ("vx", last_v)
                prev_tile_mm = mmt
        last_vx = max(pm_ev)
        return ("vx", last_v), last_vx, prev_tile_mm

    dq_state = {"sa_free": 0, "pm_free": [0, 0], "pmi": 0, "wnat_free": 0,
                "last_vx": 0, "last_tm": 0, "last_sp": 0, "f1st_free": 0}

    def emit_dequant_chunk(kind, oc):
        sa = arv(OFF_SA, F32, [128, 1024])
        wnat = arv(OFF_WDNAT, BF16, [128, 1024])
        tmpf = arv(OFF_TAIL2, F32, [128, 1024])
        ti = arv(OFF_TAIL2, F32, [128, 1024])
        WAIT("scalar", "vx", dq_state["sa_free"])
        if kind == "wd":
            r0, c0 = (oc // 3) * 128, (oc % 3) * 1024
            lt = TICK("scalar", lambda e, r0=r0, c0=c0: e.dma_start(
                out=sa, in_=wd.ap()[r0:r0 + 128, c0:c0 + 1024]), "l2", 16)
        else:
            dram = wu if kind == "wu" else wg
            lt = TICK("scalar", lambda e, d=dram, oc=oc: e.dma_start(
                out=sa, in_=d.ap()[oc * 128:(oc + 1) * 128, :]), "l2", 16)
        WAIT("vector", "l2", lt)
        WAIT("vector", "tm", dq_state["wnat_free"])
        WAIT("vector", "tm", dq_state.get("ipk_free", 0))
        sa3 = sa.rearrange("p (a b) -> p a b", b=32)
        vsync(lambda e: e.reduce_max(wbmax[:, :], sa3, axis=AX.X, apply_absolute_value=True))
        vsync(lambda e: e.tensor_scalar(wbs, wbmax[:, :], 1.0 / QMAXW, 1e-12, ALU.mult, ALU.max))
        ivt = vsync(lambda e: e.reciprocal(wbinv, wbs))
        WAIT("tensor", "vx", ivt)
        WAIT("tensor", "vx", dq_state.get("pmr_free", 0))
        trt0 = TICK("tensor", lambda e: e.transpose(pmR[0:64, 0:128], invpack[:, :], identf[:, :]), "tm")
        dq_state["ipk_free"] = trt0
        WAIT("vector", "tm", trt0)
        evt = TICK("vector", lambda e: e.tensor_copy(sT_sb[:, :], pmR[0:64, 0:128]), "vx")
        prev_v = evt
        for half in range(2):
            WAIT("tensor", "vx", prev_v)
            mmt = TICK("tensor", lambda e, h=half: e.matmul(
                pmR[:, :], sT_sb[0:32, :], expnd[0:32, h * 512:(h + 1) * 512],
                start=True, stop=True), "tm")
            WAIT("vector", "tm", mmt)
            prev_v = TICK("vector", lambda e, h=half: e.tensor_tensor(
                tmpf[:, h * 512:(h + 1) * 512], sa[:, h * 512:(h + 1) * 512],
                pmR[:, :], ALU.mult), "vx")
        sa_f = TICK("vector", lambda e: e.tensor_scalar(ti, tmpf, MAGIC, None, ALU.add), "vx")
        dq_state["sa_free"] = sa_f
        prev_v = sa_f
        for half in range(2):
            WAIT("tensor", "vx", prev_v)
            mmt = TICK("tensor", lambda e, h=half: e.matmul(
                pmR[:, :], sT_sb[32:64, :], expnd[32:64, h * 512:(h + 1) * 512],
                start=True, stop=True, tile_position=(32, 0)), "tm")
            WAIT("vector", "tm", mmt)
            prev_v = TICK("vector", lambda e, h=half: e.scalar_tensor_tensor(
                wnat[:, h * 512:(h + 1) * 512], ti[:, h * 512:(h + 1) * 512],
                -MAGIC, pmR[:, :], ALU.add, ALU.mult), "vx")
        dq_state["pmr_free"] = prev_v
        vt2 = prev_v
        dq_state["last_vx"] = vt2
        trts = []
        for grp in range(2):
            WAIT("tensor", "vx", vt2)
            WAIT("tensor", "vx", dq_state["pm_free"][0])
            trt = 0
            for ic in range(4):
                icg = grp * 4 + ic
                fn = (lambda e, i=icg, j=ic: e.transpose(
                    pmA[:, j * 128:(j + 1) * 128], wnat[:, i * 128:(i + 1) * 128], ident_sb))
                trt = TICK("tensor", fn, "tm") if ic == 3 else NOTICK("tensor", fn) or trt
            dq_state["wnat_free"] = trt
            dq_state["last_tm"] = trt
            WAIT("vector", "tm", trt)
            if kind in ("wu", "wg"):
                tgt = arv(OFF_WUPT if kind == "wu" else OFF_WGT, BF16, [128, 8, 3072])
                ev = TICK("vector", lambda e, t=tgt, oc=oc, g=grp: e.tensor_copy(
                    t[:, g * 4:(g + 1) * 4, oc * 128:(oc + 1) * 128],
                    pmA[:, :].rearrange("p (a b) -> p a b", b=128)), "vx")
                dq_state["pm_free"][0] = ev
                dq_state["last_vx"] = ev
            else:
                stg = arv(OFF_A, BF16, [128, 1024])
                WAIT("vector", "sp", dq_state["f1st_free"] if grp == 0 else 0)
                ev = TICK("vector", lambda e, s=stg, g=grp: e.tensor_copy(
                    s[:, g * 512:(g + 1) * 512], pmA[:, :]), "vx")
                dq_state["pm_free"][0] = ev
                dq_state["last_vx"] = ev
        if kind == "wd":
            stg = arv(OFF_A, BF16, [128, 1024])
            WAIT("gpsimd", "vx", dq_state["last_vx"])
            icg, ob = (oc % 3) * 8, oc // 3
            spt = TICK("gpsimd", lambda e, s=stg, icg=icg, ob=ob: e.dma_start(
                out=sp_wd.ap()[icg:icg + 8, :, ob * 128:(ob + 1) * 128].transpose([1, 0, 2]),
                in_=s.rearrange("p (a b) -> p a b", b=128)), "sp", 16)
            dq_state["f1st_free"] = spt
            dq_state["last_sp"] = spt

    def p2(qkv_sc, qkv_vx, qkv_tm):
        units = [("wd", oc) for oc in range(24)] + \
                [("wu", oc) for oc in range(NOC)] + \
                [("wg", oc) for oc in range(NOC)]
        dq_iter = iter(units)
        dq_after = {}
        for i in range(1, 11):
            dq_after[i] = 2
        dq_after[11] = 4
        for i in range(12, 16):
            dq_after[i] = 6
        dq_state["pm_free"] = [qkv_vx, qkv_vx]

        PT = [arv(OFF_PT0, BF16, [128, 1024]), arv(OFF_PT1, BF16, [128, 1024])]
        rb64 = arv(OFF_RB64, F32, [64, 2048])
        denln = arv(OFF_DENLN, F32, [1, 2048], p0=64)
        OBT = [arv(OFF_OBT0, BF16, [64, 1024]), arv(OFF_OBT1, BF16, [64, 1024])]
        Ov = arv(OFF_O, BF16, [128, 4, 4096])

        pt_cons = [0, 0]
        pst_free, pst_sem = 0, "sc"
        acc_sem, acc_free = qkv_sc
        obt_mv = [0, 0]
        exp_i = 0
        cc_ticks = []
        blk = 0
        avt = 0
        for hp in range(4):
            Kv = arv(off_K(hp), BF16, [128, 4096])
            Qv = arv(off_Q(hp), BF16, [128, 4096])
            for g in range(4):
                qts = (2 * g, 2 * g + 1)
                for kb in range(4 * qts[1] + 4):
                    for qi, qt in enumerate(qts):
                        if kb > 4 * qt + 3:
                            continue
                        WAIT("tensor", pst_sem, pst_free)
                        stt = 0
                        for hi in range(2):
                            fn = (lambda e, hi=hi, kb=kb, qt=qt, Kv=Kv, Qv=Qv:
                                  e.matmul(pst[:, hi * 512:(hi + 1) * 512],
                                           Kv[hi * 64:(hi + 1) * 64, kb * 128:(kb + 1) * 128],
                                           Qv[hi * 64:(hi + 1) * 64, qt * 512:(qt + 1) * 512],
                                           start=True, stop=True, tile_position=(hi * 64, 0)))
                            stt = TICK("tensor", fn, "tm") if hi == 1 else NOTICK("tensor", fn) or stt
                        pp = exp_i % 2
                        exp_i += 1
                        WAIT("scalar", "tm", stt)
                        WAIT("scalar", "tm", pt_cons[pp])
                        ext = TICK("scalar", lambda e, pp=pp: e.activation(
                            PT[pp], pst[:, :], AF.Exp, scale=CL(CELL_SCALE8), bias=CL(CELL_NEGC)), "sc")
                        pst_free, pst_sem = ext, "sc"
                        rdy_sem, rdy = "sc", ext
                        if kb >= 4 * qt:
                            off = kb - 4 * qt
                            WAIT("vector", "sc", ext)
                            mkt = TICK("vector", lambda e, pp=pp, off=off: e.tensor_tensor(
                                PT[pp], PT[pp], cmask_sb[:, off, :], ALU.mult), "vx")
                            rdy_sem, rdy = "vx", mkt
                        WAIT("tensor", rdy_sem, rdy)
                        if kb == 0:
                            WAIT("tensor", acc_sem, acc_free)
                        for hi in range(2):
                            slot = qi if hi == 0 else 2 + qi
                            vstat = arv(off_V(hp), BF16, [128, 2, 32, 65])[:, hi, kb, :]
                            fn = (lambda e, hi=hi, pp=pp, slot=slot, vs=vstat, kb=kb, qt=qt:
                                  e.matmul(pq[0:65, slot * 512:(slot + 1) * 512], vs,
                                           PT[pp][:, hi * 512:(hi + 1) * 512],
                                           start=(kb == 0), stop=(kb == 4 * qt + 3)))
                            avt = TICK("tensor", fn, "tm") if hi == 1 else NOTICK("tensor", fn) or avt
                        pt_cons[pp] = avt
                WAIT("scalar", "tm", avt)
                NOTICK("scalar", lambda e: e.activation(denln, pq[64:65, :], AF.Ln))
                det = TICK("scalar", lambda e: e.activation(denln, denln, AF.Exp, scale=-1.0), "sc")
                WAIT("sync", "sc", det)
                WAIT("sync", acc_sem, acc_free)
                wt_ = TICK("sync", lambda e: e.dma_start(out=rb_d.ap(), in_=denln), "ld", 16)
                WAIT("sync", "ld", wt_)
                gbt = TICK("sync", lambda e: e.dma_start(
                    out=rb64, in_=rb_d.ap().partition_broadcast(64).opt()), "ld", 16)
                obi = g % 2
                WAIT("vector", "ld", gbt)
                WAIT("vector", "sp", obt_mv[obi])
                NOTICK("vector", lambda e, hp=hp, g=g: e.tensor_tensor(
                    Ov[0:64, hp, g * 1024:(g + 1) * 1024], pq[0:64, 0:1024], rb64[:, 0:1024], ALU.mult))
                acc_free = TICK("vector", lambda e, obi=obi: e.tensor_tensor(
                    OBT[obi], pq[0:64, 1024:2048], rb64[:, 1024:2048], ALU.mult), "vx")
                acc_sem = "vx"
                WAIT("gpsimd", "vx", acc_free)
                obt_mv[obi] = TICK("gpsimd", lambda e, hp=hp, g=g, obi=obi: e.dma_start(
                    out=Ov[64:128, hp, g * 1024:(g + 1) * 1024], in_=OBT[obi]), "sp", 16)
                for _ in range(dq_after.get(blk, 0)):
                    u = next(dq_iter, None)
                    if u:
                        emit_dequant_chunk(*u)
                blk += 1
            WAIT("sync", "vx", acc_free)
            WAIT("sync", "sp", obt_mv[0])
            WAIT("sync", "sp", obt_mv[1])
            sndt = TICK("sync", lambda e, hp=hp: e.dma_start(
                out=cc_ex_in[hp].ap(), in_=Ov[:, hp, :]), "ld", 16)
            WAIT("gpsimd", "ld", sndt)
            cct = TICK("gpsimd", lambda e, hp=hp: e.collective_compute(
                "AllGather", ALU.bypass,
                replica_groups=[[0, 1], [2, 3], [4, 5], [6, 7]],
                ins=[cc_ex_in[hp].ap().opt()],
                outs=[cc_ex_out[hp].ap().opt()]), "cc")
            cc_ticks.append(cct)
        WAIT("vector", "ld", sndt)
        WAIT("tensor", "ld", sndt)
        for u in dq_iter:
            emit_dequant_chunk(*u)
        return cc_ticks

    def barrier(bi, pay, vx_ready):
        vt = 0
        WAIT("tensor", "vx", vx_ready)
        trt = 0
        for i, (slot, mx) in enumerate(pay):
            trt = TICK("tensor", lambda e, m=mx, i=i: e.transpose(
                pst[0:1, i * 128:(i + 1) * 128], m[:, :], identf[:, :]), "tm")
        for i, (slot, mx) in enumerate(pay):
            WAIT("vector", "tm", trt)
            vt = TICK("vector", lambda e, s=slot, i=i: e.reduce_max(
                rowbuf[0:1, s:s + 1], pst[0:1, i * 128:(i + 1) * 128], axis=AX.X), "vx")
        WAIT("sync", "vx", vt)
        lt = TICK("sync", lambda e, bi=bi: e.dma_start(out=cc_b_in[bi].ap(), in_=rowbuf[0:1, 0:8]), "ld", 16)
        WAIT("gpsimd", "ld", lt)
        cct = TICK("gpsimd", lambda e, bi=bi: e.collective_compute(
            "AllGather", ALU.bypass, replica_groups=[list(range(8))],
            ins=[cc_b_in[bi].ap().opt()], outs=[cc_b_out[bi].ap().opt()]), "cc")
        WAIT("sync", "cc", cct)
        lt2 = TICK("sync", lambda e, bi=bi: e.dma_start(
            out=rowcc[0:1, :], in_=cc_b_out[bi].ap().rearrange("a b -> (a b)").unsqueeze(0)), "ld", 16)
        WAIT("vector", "ld", lt2)
        vt = 0
        for slot, _ in pay:
            rview = rowcc[0:1, :].rearrange("p (r s) -> p r s", s=8)[:, :, slot]
            vt = TICK("vector", lambda e, s=slot, rv=rview: e.reduce_max(
                rowbuf[0:1, s:s + 1], rv, axis=AX.X), "vx")
            WAIT("vector", "vx", vt)
        WAIT("sync", "vx", vt)
        wt2 = TICK("sync", lambda e: e.dma_start(out=cdram.ap()[0:1, 0:8], in_=rowbuf[0:1, 0:8]), "ld", 16)
        WAIT("sync", "ld", wt2)
        bt2 = TICK("sync", lambda e: e.dma_start(
            out=cells[:, 24:32],
            in_=cdram.ap()[0:1, 0:8].partition_broadcast(128).opt()), "ld", 16)
        WAIT("vector", "ld", bt2)

    def rowop(fn):
        v = TICK("vector", fn, "vx")
        WAIT("vector", "vx", v)
        return v

    def vsync(fn):
        v = TICK("vector", fn, "vx")
        WAIT("vector", "vx", v)
        return v

    def bcast_cells(lo, hi, vx_ready):
        WAIT("sync", "vx", vx_ready)
        wt_ = TICK("sync", lambda e, lo=lo, hi=hi: e.dma_start(
            out=cdram.ap()[0:1, lo:hi], in_=cellrow[0:1, lo:hi]), "ld", 16)
        WAIT("sync", "ld", wt_)
        return TICK("sync", lambda e, lo=lo, hi=hi: e.dma_start(
            out=cells[:, lo:hi],
            in_=cdram.ap()[0:1, lo:hi].partition_broadcast(128).opt()), "ld", 16)

    def p3(cc_ticks):
        Afull = [arv(AFULL_OFFS[c], BF16, [128, 2048]) for c in range(8)]
        woutbf = arv(OFF_WOUTBF, BF16, [128, 8, 1024])
        wst = arv(OFF_YSB, F32, [128, 1024])
        ysb = arv(OFF_YSB, F32, [128, 1024])
        xmidt = arv(OFF_XMIDT, F32, [128, 1024])
        xln2t = arv(OFF_XLN2T, F32, [128, 1024])
        xh = [arv(OFF_XH0, F32, [128, 1024]), arv(OFF_XH1, F32, [128, 1024])]

        WAIT("sync", "vx", dq_state["last_vx"])
        WAIT("sync", "tm", dq_state["last_tm"])
        WAIT("sync", "sp", dq_state["last_sp"])
        WAIT("sync", "l2", dq_state.get("invpack_free", 0))
        atmp = arv(OFF_XH0, BF16, [128, 2048])
        af_ld = 0
        blend_prev = 0
        for c in range(8):
            slot, hp = c // 4, c % 4
            WAIT("sync", "cc", cc_ticks[hp])
            WAIT("sync", "vx", blend_prev)
            lo_t = TICK("sync", lambda e, c=c, slot=slot, hp=hp: e.dma_start(
                out=Afull[c], in_=cc_ex_out[hp].ap()[slot, :, 0:TPC]), "ld", 16)
            hi_t = TICK("sync", lambda e, c=c, slot=slot, hp=hp: e.dma_start(
                out=atmp, in_=cc_ex_out[hp].ap()[slot, :, TPC:2 * TPC]), "ld", 16)
            WAIT("vector", "ld", hi_t)
            NOTICK("vector", lambda e, c=c: e.tensor_scalar(
                Afull[c], Afull[c], hsel[:, 0:1], None, ALU.mult))
            blend_prev = TICK("vector", lambda e, c=c: e.scalar_tensor_tensor(
                Afull[c], atmp, hsel[:, 1:2], Afull[c], ALU.mult, ALU.add), "vx")
            af_ld = blend_prev
        cast_t = 0
        for dc in range(8):
            WAIT("sync", "vx", cast_t)
            lt = TICK("sync", lambda e, dc=dc: e.dma_start(
                out=wst, in_=wout_t.ap()[dc * 128:(dc + 1) * 128, :]), "ld", 16)
            WAIT("vector", "ld", lt)
            cast_t = TICK("vector", lambda e, dc=dc: e.tensor_copy(woutbf[:, dc, :], wst), "vx")

        WAIT("tensor", "vx", af_ld)
        WAIT("tensor", "vx", cast_t)
        spill_t = 0
        ysb_sem, ysb_free = "vx", cast_t
        mm_free = [("vx", 0), ("vx", 0)]
        xh_free = [0, 0]
        amx = 0
        for tt in range(NTT):
            pb = tt % 2
            psv = pst if pb == 0 else pq
            WAIT("sync", "vx", xh_free[pb])
            xht = TICK("sync", lambda e, tt=tt, b=pb: e.dma_start(
                out=xh[b], in_=xhalf.ap()[tt * 128:(tt + 1) * 128, :]), "ld", 16)
            WAIT("tensor", mm_free[pb][0], mm_free[pb][1])
            mmt = 0
            for oh in range(2):
                for dc in range(8):
                    fn = (lambda e, ps=psv, oh=oh, dc=dc, tt=tt:
                          e.matmul(ps[:, oh * 512:(oh + 1) * 512],
                                   Afull[dc][:, tt * 128:(tt + 1) * 128],
                                   woutbf[:, dc, oh * 512:(oh + 1) * 512],
                                   start=(dc == 0), stop=(dc == 7)))
                    mmt = TICK("tensor", fn, "tm") if (dc == 7 and oh == 1) else NOTICK("tensor", fn) or mmt
            WAIT("vector", "tm", mmt)
            WAIT("vector", ysb_sem, ysb_free)
            vt = vsync(lambda e, ps=psv: e.tensor_tensor(ysb, ps[:, 0:1024], bout_rep, ALU.add))
            mm_free[pb] = ("vx", vt)
            NOTICK("vector", lambda e: e.bn_stats(bn6[:, 0:6], ysb[:, 0:512]))
            vsync(lambda e: e.bn_stats(bn6[:, 6:12], ysb[:, 512:1024]))
            vt = vsync(lambda e: e.bn_aggr(bn2[:, :], bn6[:, :]))
            WAIT("scalar", "vx", vt)
            ln_tail(bn2[:, 1:2], bn2[:, 0:1])
            st_ = TICK("scalar", lambda e: e.activation(ysb, ysb, AF.Identity, bias=nmrs[:, :], scale=r1v[:, :]), "sc")
            WAIT("vector", "sc", st_)
            WAIT("vector", "ld", xht)
            WAIT("vector", "sp", spill_t)
            xh_free[pb] = vsync(lambda e, b=pb: e.tensor_tensor(xmidt, ysb, xh[b], ALU.add))
            ysb_sem, ysb_free = "vx", xh_free[pb]
            NOTICK("vector", lambda e: e.bn_stats(bn6[:, 0:6], xmidt[:, 0:512]))
            vsync(lambda e: e.bn_stats(bn6[:, 6:12], xmidt[:, 512:1024]))
            vt = vsync(lambda e: e.bn_aggr(bn2[:, :], bn6[:, :]))
            WAIT("scalar", "vx", vt)
            ln_tail(bn2[:, 1:2], bn2[:, 0:1])
            WAIT("scalar", "sp", spill_t)
            st_ = TICK("scalar", lambda e: e.activation(xln2t, xmidt, AF.Identity, bias=nmrs[:, :], scale=r1v[:, :]), "sc")
            WAIT("vector", "sc", st_)
            vsync(lambda e: e.reduce_max(tmx[:, :], xln2t, axis=AX.X, apply_absolute_value=True))
            amx = TICK("vector", lambda e: e.tensor_tensor(mx_x[:, :], mx_x[:, :], tmx[:, :], ALU.max), "vx")
            WAIT("gpsimd", "vx", amx)
            WAIT("gpsimd", "sc", st_)
            TICK("gpsimd", lambda e, tt=tt: e.dma_start(out=sp_xmid.ap()[tt], in_=xmidt), "sp", 16)
            spill_t = TICK("gpsimd", lambda e, tt=tt: e.dma_start(out=sp_xln2.ap()[tt], in_=xln2t), "sp", 16)
        barrier(0, [(0, mx_x)], amx)
        rowop(lambda e: e.tensor_scalar(rowbuf[0:1, ROW_SX:ROW_SX + 1], rowbuf[0:1, 0:1],
                                        1.0 / QMAXA, 1e-12, ALU.mult, ALU.max))
        rv = rowop(lambda e: e.reciprocal(cellrow[0:1, CELL_INVSX:CELL_INVSX + 1],
                                          rowbuf[0:1, ROW_SX:ROW_SX + 1]))
        g1 = bcast_cells(CELL_INVSX, CELL_INVSX + 1, rv)
        return g1, spill_t

    def p4(b1_gp, p3_sp):
        Xq = arv(XQ_OFF, BF16, [128, 8, 2048])
        xl2 = [arv(OFF_XL2LD0, F32, [128, 1024]), arv(OFF_XL2LD1, F32, [128, 1024])]
        xqi = arv(OFF_XQI, F32, [128, 1024])
        xqbf = arv(OFF_XQBF, BF16, [128, 1024])
        upT = arv(OFF_WUPT, BF16, [128, 8, 3072])
        gT = arv(OFF_WGT, BF16, [128, 8, 3072])
        stg = [arv(OFF_F1ST0, BF16, [128, 2048]), arv(OFF_F1ST1, BF16, [128, 2048])]

        WAIT("sync", "sp", p3_sp)
        ld_free = [0, 0]
        pm_free = [dq_state["pm_free"][0], dq_state["pm_free"][1]]
        prev_trt = dq_state["last_tm"]
        evt = 0
        for tt in range(NTT):
            b = tt % 2
            WAIT("sync", "vx", ld_free[b])
            lt = TICK("sync", lambda e, tt=tt, b=b: e.dma_start(out=xl2[b], in_=sp_xln2.ap()[tt]), "ld", 16)
            WAIT("vector", "ld", lt)
            WAIT("vector", "ld", b1_gp)
            WAIT("vector", "tm", prev_trt)
            ld_free[b] = TICK("vector", lambda e, b=b: e.tensor_scalar(
                xqi, xl2[b], CL(CELL_INVSX), MAGIC, ALU.mult, ALU.add), "vx")
            vq = TICK("vector", lambda e: e.tensor_scalar(xqbf, xqi, -MAGIC, None, ALU.add), "vx")
            WAIT("tensor", "vx", vq)
            for grp in range(2):
                WAIT("tensor", "vx", pm_free[0])
                trt = 0
                for ic in range(4):
                    fn = (lambda e, i=grp * 4 + ic, j=ic: e.transpose(
                        pmA[:, j * 128:(j + 1) * 128], xqbf[:, i * 128:(i + 1) * 128], ident_sb))
                    trt = TICK("tensor", fn, "tm") if ic == 3 else NOTICK("tensor", fn) or trt
                prev_trt = trt
                WAIT("vector", "tm", trt)
                pm_free[0] = TICK("vector", lambda e, tt=tt, g=grp: e.tensor_copy(
                    Xq[:, g * 4:(g + 1) * 4, tt * 128:(tt + 1) * 128],
                    pmA[:, :].rearrange("p (a b) -> p a b", b=128)), "vx")
            evt = pm_free[0]
        sp_map = {0: sp_pu, 1: sp_pg}
        wmap = {0: upT, 1: gT}
        mmax = {0: mx_u, 1: mx_g}
        slot_free = {}
        stg_free = [0, 0]
        sp_t = 0
        vamx = 0
        WAIT("tensor", "vx", evt)
        WAIT("tensor", "vx", dq_state["last_vx"])
        for wi in range(2):
            for oc in range(NOC):
                si = oc % 2
                for ttt in range(4):
                    WAIT("tensor", "sc", slot_free.get(ttt, 0))
                    mmt = 0
                    for dc in range(8):
                        fn = (lambda e, w=wmap[wi], dc=dc, oc=oc, ttt=ttt:
                              e.matmul(pq[:, ttt * 512:(ttt + 1) * 512],
                                       w[:, dc, oc * 128:(oc + 1) * 128],
                                       Xq[:, dc, ttt * 512:(ttt + 1) * 512],
                                       start=(dc == 0), stop=(dc == 7)))
                        mmt = TICK("tensor", fn, "tm") if dc == 7 else NOTICK("tensor", fn) or mmt
                    WAIT("scalar", "tm", mmt)
                    WAIT("scalar", "sp", stg_free[si])
                    slot_free[ttt] = TICK("scalar", lambda e, ttt=ttt, si=si: e.copy(
                        stg[si][:, ttt * 512:(ttt + 1) * 512], pq[:, ttt * 512:(ttt + 1) * 512]), "sc")
                WAIT("vector", "sc", slot_free[3])
                vsync(lambda e, si=si: e.reduce_max(tmx[:, :], stg[si], axis=AX.X,
                                                    apply_absolute_value=True))
                vamx = TICK("vector", lambda e, wi=wi: e.tensor_tensor(
                    mmax[wi][:, :], mmax[wi][:, :], tmx[:, :], ALU.max), "vx")
                WAIT("gpsimd", "vx", vamx)
                sp_t = TICK("gpsimd", lambda e, wi=wi, oc=oc, si=si: e.dma_start(
                    out=sp_map[wi].ap()[oc], in_=stg[si]), "sp", 16)
                stg_free[si] = sp_t
        barrier(1, [(0, mx_u), (1, mx_g)], vamx)
        rowop(lambda e: e.tensor_tensor(rowbuf[0:1, ROW_TA:ROW_TA + 1], rowbuf[0:1, 0:1],
                                        rowbuf[0:1, ROW_SX:ROW_SX + 1], ALU.mult))
        rowop(lambda e: e.tensor_scalar(rowbuf[0:1, ROW_SU:ROW_SU + 1], rowbuf[0:1, ROW_TA:ROW_TA + 1],
                                        1.0 / QMAXA, 1e-12, ALU.mult, ALU.max))
        rowop(lambda e: e.tensor_tensor(rowbuf[0:1, ROW_TA:ROW_TA + 1], rowbuf[0:1, 1:2],
                                        rowbuf[0:1, ROW_SX:ROW_SX + 1], ALU.mult))
        rowop(lambda e: e.tensor_scalar(rowbuf[0:1, ROW_SG:ROW_SG + 1], rowbuf[0:1, ROW_TA:ROW_TA + 1],
                                        1.0 / QMAXA, 1e-12, ALU.mult, ALU.max))
        rowop(lambda e: e.reciprocal(rowbuf[0:1, ROW_TA:ROW_TA + 1], rowbuf[0:1, ROW_SG:ROW_SG + 1]))
        rowop(lambda e: e.tensor_tensor(cellrow[0:1, CELL_C1:CELL_C1 + 1], rowbuf[0:1, ROW_SX:ROW_SX + 1],
                                        rowbuf[0:1, ROW_TA:ROW_TA + 1], ALU.mult))
        rowop(lambda e: e.tensor_copy(cellrow[0:1, CELL_SG:CELL_SG + 1], rowbuf[0:1, ROW_SG:ROW_SG + 1]))
        rv = rowop(lambda e: e.tensor_scalar(cellrow[0:1, CELL_BSG:CELL_BSG + 1], rowbuf[0:1, ROW_SG:ROW_SG + 1],
                                             -B_ACT, None, ALU.mult))
        g = bcast_cells(CELL_C1, CELL_BSG + 1, rv)
        return g, sp_t

    def p5(b2_gp, p4_sp):
        pgld = [arv(OFF_PGLD0, BF16, [128, 2048]), arv(OFF_PGLD1, BF16, [128, 2048])]
        t1i = arv(OFF_T1I, F32, [128, 2048])
        sig = arv(OFF_SIG, F32, [128, 2048])
        t2i = arv(OFF_T2I, F32, [128, 2048])
        t1f = arv(OFF_T1F, F32, [128, 2048])
        t3 = [arv(OFF_T3A, F32, [128, 2048]), arv(OFF_T3B, F32, [128, 2048])]
        WAIT("sync", "sp", p4_sp)
        ld_free = [0, 0]
        t3_free = [0, 0]
        sp_t = 0
        vamx = 0
        for oc in range(NOC):
            b = oc % 2
            WAIT("sync", "vx", ld_free[b])
            lt = TICK("sync", lambda e, oc=oc, b=b: e.dma_start(out=pgld[b], in_=sp_pg.ap()[oc]), "ld", 16)
            WAIT("vector", "ld", lt)
            WAIT("vector", "ld", b2_gp)
            NOTICK("vector", lambda e, b=b: e.tensor_scalar(
                t1i, pgld[b], CL(CELL_C1), MAGIC, ALU.mult, ALU.add))
            vt1 = TICK("vector", lambda e: e.tensor_scalar(t1f, t1i, -MAGIC, None, ALU.add), "vx")
            WAIT("scalar", "vx", vt1)
            st_ = TICK("scalar", lambda e: e.activation(
                sig, t1f, AF.Sigmoid, scale=CL(CELL_SG)), "sc")
            ld_free[b] = vt1
            WAIT("vector", "sc", st_)
            NOTICK("vector", lambda e: e.tensor_scalar(t2i, sig, 65536.0, MAGIC, ALU.mult, ALU.add))
            NOTICK("vector", lambda e: e.tensor_scalar(t2i, t2i, -MAGIC, 65535.0, ALU.add, ALU.min))
            WAIT("vector", "sp", t3_free[b])
            TICK("vector", lambda e, b=b: e.tensor_tensor(t3[b], t1f, t2i, ALU.mult), "vx")
            vsync(lambda e, b=b: e.reduce_max(tmx[:, :], t3[b], axis=AX.X,
                                                apply_absolute_value=True))
            vamx = TICK("vector", lambda e: e.tensor_tensor(mx_3[:, :], mx_3[:, :], tmx[:, :], ALU.max), "vx")
            WAIT("gpsimd", "vx", vamx)
            sp_t = TICK("gpsimd", lambda e, oc=oc, b=b: e.dma_start(out=sp_t3.ap()[oc], in_=t3[b]), "sp", 16)
            t3_free[b] = sp_t
        barrier(2, [(0, mx_3)], vamx)
        rowop(lambda e: e.tensor_scalar(rowbuf[0:1, ROW_TA:ROW_TA + 1], rowbuf[0:1, ROW_SG:ROW_SG + 1],
                                        1.0 / 65536.0, None, ALU.mult))
        rowop(lambda e: e.tensor_tensor(rowbuf[0:1, ROW_TB:ROW_TB + 1], rowbuf[0:1, 0:1],
                                        rowbuf[0:1, ROW_TA:ROW_TA + 1], ALU.mult))
        rowop(lambda e: e.tensor_scalar(rowbuf[0:1, ROW_S2:ROW_S2 + 1], rowbuf[0:1, ROW_TB:ROW_TB + 1],
                                        1.0 / QMAXA, 1e-12, ALU.mult, ALU.max))
        rowop(lambda e: e.reciprocal(rowbuf[0:1, ROW_TB:ROW_TB + 1], rowbuf[0:1, ROW_S2:ROW_S2 + 1]))
        rowop(lambda e: e.tensor_tensor(cellrow[0:1, CELL_C3:CELL_C3 + 1], rowbuf[0:1, ROW_TA:ROW_TA + 1],
                                        rowbuf[0:1, ROW_TB:ROW_TB + 1], ALU.mult))
        rowop(lambda e: e.reciprocal(rowbuf[0:1, ROW_TA:ROW_TA + 1], rowbuf[0:1, ROW_SU:ROW_SU + 1]))
        rv = rowop(lambda e: e.tensor_tensor(cellrow[0:1, CELL_C4:CELL_C4 + 1], rowbuf[0:1, ROW_SX:ROW_SX + 1],
                                             rowbuf[0:1, ROW_TA:ROW_TA + 1], ALU.mult))
        g = bcast_cells(CELL_C3, CELL_C4 + 1, rv)
        return g, sp_t

    def p6(b3_gp, p5_sp):
        puld = [arv(OFF_PULD0, BF16, [128, 2048]), arv(OFF_PULD1, BF16, [128, 2048])]
        t3ld = [arv(OFF_T3LD0, F32, [128, 2048]), arv(OFF_T3LD1, F32, [128, 2048])]
        t4i = arv(OFF_T4I, F32, [128, 2048])
        t5i = arv(OFF_T5I, F32, [128, 2048])
        t5f = arv(OFF_T5F, F32, [128, 2048])
        t6 = [arv(OFF_T6A, F32, [128, 2048]), arv(OFF_T6B, F32, [128, 2048])]
        WAIT("sync", "sp", p5_sp)
        ld_free = [0, 0]
        t6_free = [0, 0]
        sp_t = 0
        vamx = 0
        for oc in range(NOC):
            b = oc % 2
            WAIT("sync", "vx", ld_free[b])
            TICK("sync", lambda e, oc=oc, b=b: e.dma_start(out=puld[b], in_=sp_pu.ap()[oc]), "ld", 16)
            lt = TICK("sync", lambda e, oc=oc, b=b: e.dma_start(out=t3ld[b], in_=sp_t3.ap()[oc]), "ld", 16)
            WAIT("vector", "ld", lt)
            WAIT("vector", "ld", b3_gp)
            NOTICK("vector", lambda e, b=b: e.tensor_scalar(
                t4i, t3ld[b], CL(CELL_C3), MAGIC, ALU.mult, ALU.add))
            ld_free[b] = TICK("vector", lambda e, b=b: e.tensor_scalar(
                t5i, puld[b], CL(CELL_C4), MAGIC, ALU.mult, ALU.add), "vx")
            WAIT("scalar", "vx", ld_free[b])
            st_ = TICK("scalar", lambda e: e.activation(t5f, t5i, AF.Identity, bias=CL(CELL_NEGMG)), "sc")
            WAIT("vector", "sc", st_)
            WAIT("vector", "sp", t6_free[b])
            TICK("vector", lambda e, b=b: e.scalar_tensor_tensor(
                t6[b], t4i, -MAGIC, t5f, ALU.add, ALU.mult), "vx")
            vsync(lambda e, b=b: e.reduce_max(tmx[:, :], t6[b], axis=AX.X,
                                                apply_absolute_value=True))
            vamx = TICK("vector", lambda e: e.tensor_tensor(mx_6[:, :], mx_6[:, :], tmx[:, :], ALU.max), "vx")
            WAIT("gpsimd", "vx", vamx)
            sp_t = TICK("gpsimd", lambda e, oc=oc, b=b: e.dma_start(out=sp_t6.ap()[oc], in_=t6[b]), "sp", 16)
            t6_free[b] = sp_t
        barrier(3, [(0, mx_6)], vamx)
        rowop(lambda e: e.tensor_tensor(rowbuf[0:1, ROW_TA:ROW_TA + 1], rowbuf[0:1, ROW_S2:ROW_S2 + 1],
                                        rowbuf[0:1, ROW_SU:ROW_SU + 1], ALU.mult))
        rowop(lambda e: e.tensor_tensor(rowbuf[0:1, ROW_TB:ROW_TB + 1], rowbuf[0:1, 0:1],
                                        rowbuf[0:1, ROW_TA:ROW_TA + 1], ALU.mult))
        rowop(lambda e: e.tensor_scalar(rowbuf[0:1, ROW_SO:ROW_SO + 1], rowbuf[0:1, ROW_TB:ROW_TB + 1],
                                        1.0 / QMAXA, 1e-12, ALU.mult, ALU.max))
        rowop(lambda e: e.reciprocal(rowbuf[0:1, ROW_TB:ROW_TB + 1], rowbuf[0:1, ROW_SO:ROW_SO + 1]))
        rowop(lambda e: e.tensor_tensor(cellrow[0:1, CELL_C5:CELL_C5 + 1], rowbuf[0:1, ROW_TA:ROW_TA + 1],
                                        rowbuf[0:1, ROW_TB:ROW_TB + 1], ALU.mult))
        rowop(lambda e: e.tensor_copy(cellrow[0:1, CELL_SO:CELL_SO + 1], rowbuf[0:1, ROW_SO:ROW_SO + 1]))
        rv = rowop(lambda e: e.tensor_tensor(cellrow[0:1, CELL_SO2:CELL_SO2 + 1], rowbuf[0:1, ROW_SO:ROW_SO + 1],
                                             rowbuf[0:1, ROW_SO:ROW_SO + 1], ALU.mult))
        g = bcast_cells(CELL_C5, CELL_SO2 + 1, rv)
        return g, sp_t

    def p7(b4_gp, p6_sp):
        t6ld = [arv(OFF_T6LD0, F32, [128, 2048]), arv(OFF_T6LD1, F32, [128, 2048])]
        t7i = arv(OFF_T7I, F32, [128, 2048])
        t7 = arv(OFF_T7, BF16, [128, NOC, 2048])
        wdT = arv(OFF_WDT, BF16, [128, NOC, 1024])
        xmld = [arv(OFF_XMLD0, F32, [128, 1024]), arv(OFF_XMLD1, F32, [128, 1024])]
        outb = [arv(OFF_OUTB0, F32, [128, 1024]), arv(OFF_OUTB1, F32, [128, 1024])]

        WAIT("sync", "sp", p6_sp)
        wd_ld = 0
        for ic2 in range(0, NOC, 4):
            wd_ld = TICK("sync", lambda e, ic2=ic2: e.dma_start(
                out=wdT[:, ic2:ic2 + 4, :],
                in_=sp_wd.ap()[ic2:ic2 + 4].transpose([1, 0, 2])), "ld", 16)
        ld_free = [0, 0]
        t7t = 0
        for oc in range(NOC):
            b = oc % 2
            WAIT("sync", "vx", ld_free[b])
            lt = TICK("sync", lambda e, oc=oc, b=b: e.dma_start(out=t6ld[b], in_=sp_t6.ap()[oc]), "ld", 16)
            WAIT("vector", "ld", lt)
            WAIT("vector", "ld", b4_gp)
            ld_free[b] = TICK("vector", lambda e, b=b: e.tensor_scalar(
                t7i, t6ld[b], CL(CELL_C5), MAGIC, ALU.mult, ALU.add), "vx")
            t7t = TICK("vector", lambda e, oc=oc: e.tensor_scalar(
                t7[:, oc, :], t7i, -MAGIC, None, ALU.add), "vx")
        WAIT("tensor", "vx", t7t)
        WAIT("tensor", "ld", wd_ld)
        xm_free = [0, 0]
        mm_free = [("vx", 0), ("vx", 0)]
        out_free = [0, 0]
        store_t = 0
        prev_dsb_rd = 0
        for tt in range(NTT):
            pb = tt % 2
            psv = pst if pb == 0 else pq
            WAIT("sync", "vx", xm_free[pb])
            xmt = TICK("sync", lambda e, tt=tt, b=pb: e.dma_start(out=xmld[b], in_=sp_xmid.ap()[tt]), "ld", 16)
            WAIT("tensor", mm_free[pb][0], mm_free[pb][1])
            mmt = 0
            for oh in range(2):
                for ic in range(NOC):
                    fn = (lambda e, ps=psv, oh=oh, ic=ic, tt=tt:
                          e.matmul(ps[:, oh * 512:(oh + 1) * 512],
                                   t7[:, ic, tt * 128:(tt + 1) * 128],
                                   wdT[:, ic, oh * 512:(oh + 1) * 512],
                                   start=(ic == 0), stop=(ic == NOC - 1)))
                    mmt = TICK("tensor", fn, "tm") if (ic == NOC - 1 and oh == 1) else NOTICK("tensor", fn) or mmt
            WAIT("vector", "tm", mmt)
            dsb = arv(OFF_T7I, F32, [128, 1024])
            WAIT("vector", "sc", prev_dsb_rd)
            vt0 = vsync(lambda e, ps=psv: e.tensor_copy(dsb, ps[:, 0:1024]))
            NOTICK("vector", lambda e: e.bn_stats(bn6[:, 0:6], dsb[:, 0:512]))
            vsync(lambda e: e.bn_stats(bn6[:, 6:12], dsb[:, 512:1024]))
            vsync(lambda e: e.bn_aggr(bn2[:, :], bn6[:, :]))
            vt = TICK("vector", lambda e: e.tensor_scalar(
                r1[:, :], bn2[:, 1:2], CL(CELL_SO2), EPS, ALU.mult, ALU.add), "vx")
            WAIT("scalar", "vx", vt)
            NOTICK("scalar", lambda e: e.activation(r1[:, :], r1[:, :], AF.Ln))
            st_ = TICK("scalar", lambda e: e.activation(r1[:, :], r1[:, :], AF.Exp, scale=CL(CELL_MHALF)), "sc")
            WAIT("vector", "sc", st_)
            vsync(lambda e: e.tensor_scalar(r1[:, :], r1[:, :], CL(CELL_SO), None, ALU.mult))
            vt = TICK("vector", lambda e: e.tensor_scalar(
                nmrs[:, :], bn2[:, 0:1], r1[:, :], -1.0, ALU.mult, ALU.mult), "vx")
            WAIT("scalar", "vx", vt)
            WAIT("scalar", "st", out_free[pb])
            dsb = arv(OFF_T7I, F32, [128, 1024])
            st_ = TICK("scalar", lambda e, pb=pb: e.activation(
                outb[pb], dsb, AF.Identity, bias=nmrs[:, :], scale=r1[:, :]), "sc")
            prev_dsb_rd = st_
            mm_free[pb] = ("vx", vt0)
            WAIT("vector", "sc", st_)
            WAIT("vector", "ld", xmt)
            xm_free[pb] = TICK("vector", lambda e, pb=pb, b=pb: e.tensor_tensor(
                outb[pb], outb[pb], xmld[b], ALU.add), "vx")
            WAIT("sync", "vx", xm_free[pb])
            store_t = TICK("sync", lambda e, tt=tt, pb=pb: e.dma_start(
                out=out.ap()[tt * 128:(tt + 1) * 128, :], in_=outb[pb]), "st", 16)
            out_free[pb] = store_t
        WAIT("sync", "st", store_t)
        return store_t

    import os as _os
    _plim = int(_os.environ.get("KERNEL_PHASE_LIMIT", "9"))
    p0()
    final_t = 0
    if _plim >= 1:
        qkv_sc, qkv_vx, qkv_tm = p1()
        if _DBG_QKV[0]:
            WAIT("sync", qkv_sc[0], qkv_sc[1])
            TICK("sync", lambda e: e.dma_start(
                out=dbg_q0.ap(), in_=arv(off_Q(0), BF16, [128, 4096])), "st", 16)
            TICK("sync", lambda e: e.dma_start(
                out=dbg_k0.ap(), in_=arv(off_K(0), BF16, [128, 4096])), "st", 16)
            TICK("sync", lambda e: e.dma_start(
                out=dbg_v0.ap(), in_=arv(off_V(0), BF16, [128, 4160])), "st", 16)
            TICK("sync", lambda e: e.dma_start(
                out=dbg_pqv.ap(), in_=arv(62592, F32, [128, 2048])), "st", 16)
            TICK("sync", lambda e: e.dma_start(
                out=dbg_xf.ap(), in_=arv(OFF_XF, BF16, [128, 4096])), "st", 16)
            dqt = TICK("sync", lambda e: e.dma_start(
                out=dbg_wv.ap(), in_=arv(OFF_WVBF, BF16, [128, 4096])), "st", 16)
            WAIT("sync", "st", dqt)
    if _plim >= 2:
        cc_ticks = p2(qkv_sc, qkv_vx, qkv_tm)
    if _plim >= 3:
        b1_gp, p3_sp = p3(cc_ticks)
    if _plim >= 4:
        b2_gp, p4_sp = p4(b1_gp, p3_sp)
    if _plim >= 5:
        b3_gp, p5_sp = p5(b2_gp, p4_sp)
    if _plim >= 6:
        b4_gp, p6_sp = p6(b3_gp, p5_sp)
    if _plim >= 7:
        final_t = p7(b4_gp, p6_sp)

    if _DBG_CELLS[0]:
        WAIT("sync", "st", final_t)
        TICK("sync", lambda e: e.dma_start(out=dbg_cells.ap(), in_=cells[:, :]), "st", 16)
        TICK("sync", lambda e: e.dma_start(out=dbg_row.ap()[0:1, :], in_=rowbuf[:, :]), "st", 16)
        TICK("sync", lambda e: e.dma_start(out=dbg_row.ap()[1:2, :], in_=cellrow[:, :]), "st", 16)
        TICK("sync", lambda e: e.dma_start(out=dbg_small.ap()[:, 0:1], in_=r1[:, :],
                                           allow_slow_non_contiguous=True), "st", 16)
        TICK("sync", lambda e: e.dma_start(out=dbg_small.ap()[:, 1:2], in_=nmrs[:, :],
                                           allow_slow_non_contiguous=True), "st", 16)
        TICK("sync", lambda e: e.dma_start(out=dbg_small.ap()[:, 2:4], in_=bn2[:, :],
                                           allow_slow_non_contiguous=True), "st", 16)
        dsbv = arv(OFF_T7I, F32, [128, 1024])
        obv = arv(OFF_OUTB0, F32, [128, 1024])
        TICK("sync", lambda e: e.dma_start(out=dbg_dsb.ap()[:, 0:1024], in_=dsbv), "st", 16)
        final_t = TICK("sync", lambda e: e.dma_start(out=dbg_dsb.ap()[:, 1024:2048], in_=obv), "st", 16)
    for name, (dst, srch) in ddbg.items():
        WAIT("sync", "st", final_t)
        final_t = TICK("sync", lambda e, d=dst, s=srch: e.dma_start(out=d.ap(), in_=s.ap()), "st", 16)
    WAIT("sync", "st", final_t)
    for (name, off, dt, nel) in taps:
        src = arv(off, dt, [128, nel])
        tmpd = nc.alloc_sbuf_tensor("dbgtmp_" + name, [128, nel], F32)
        WAIT("vector", "st", final_t)
        vt = TICK("vector", lambda e, a=tmpd, b=src: e.tensor_copy(a[:, :], b), "vx")
        WAIT("sync", "vx", vt)
        final_t = TICK("sync", lambda e, nm=name, a=tmpd: e.dma_start(out=dbg[nm].ap(), in_=a[:, :]), "st", 16)
    WAIT("sync", "st", final_t)

    block = es.enter_context(nc.Block())

    import os as _os2
    _otrunc = int(_os2.environ.get("KERNEL_OPS_TRUNC", "0")) or None

    def _mk(engine_name):
        def run(eng):
            for sq, f in ops[engine_name]:
                if _otrunc is not None and sq > _otrunc:
                    continue
                f(eng)
        return run

    block.sync(_mk("sync"))
    block.tensor(_mk("tensor"))
    block.vector(_mk("vector"))
    block.scalar(_mk("scalar"))
    block.gpsimd(_mk("gpsimd"))
    es.close()
    return nc


def _host_masks():
    m = np.zeros((128, 4, 1024), dtype=ml_dtypes.bfloat16)
    for off_i, off in enumerate((0, 128, 256, 384)):
        i = np.arange(128)[:, None]
        j = np.arange(512)[None, :]
        valid = (j >= i + off).astype(ml_dtypes.bfloat16)
        m[:, off_i, 0:512] = valid
        m[:, off_i, 512:1024] = valid
    return np.ascontiguousarray(m.reshape(128, 4096))


def kernel(L0_attn_x, L0_attn_Wqkv, L0_attn_Wout, L0_attn_bout,
           L0_ffn_w_gate, L0_ffn_w_up, L0_ffn_w_down):
    x = np.ascontiguousarray(np.asarray(L0_attn_x, dtype=np.float32))
    Wqkv = np.asarray(L0_attn_Wqkv, dtype=np.float32)
    Wout = np.asarray(L0_attn_Wout, dtype=np.float32)
    bout = np.ascontiguousarray(np.asarray(L0_attn_bout, dtype=np.float32))
    wgh = np.ascontiguousarray(np.asarray(L0_ffn_w_gate, dtype=np.float32))
    wuh = np.ascontiguousarray(np.asarray(L0_ffn_w_up, dtype=np.float32))
    wdh = np.ascontiguousarray(np.asarray(L0_ffn_w_down, dtype=np.float32))
    wout_th = np.ascontiguousarray(Wout.T)
    cmask = _host_masks()
    ident = np.ascontiguousarray(np.eye(128, dtype=ml_dtypes.bfloat16))
    ex1 = np.repeat(np.eye(32, dtype=np.float32), 32, axis=1)
    expander = np.ascontiguousarray(np.concatenate([ex1, ex1], axis=0))

    nc = build_nc(taps=tuple(_DBG_TAPS))
    in_maps = []
    for c in range(NCORE):
        b, h2 = c // 2, c % 2
        qs = np.ascontiguousarray(Wqkv[512 * h2:512 * h2 + 512, :].T)
        ks = np.ascontiguousarray(Wqkv[D + 512 * h2:D + 512 * h2 + 512, :].T)
        vs = np.ascontiguousarray(Wqkv[2 * D + 512 * h2:2 * D + 512 * h2 + 512, :].T)
        in_maps.append({
            "xb": np.ascontiguousarray(x[b]),
            "xhalf": np.ascontiguousarray(x[b, TPC * h2:TPC * h2 + TPC, :]),
            "wq_t": qs, "wk_t": ks, "wv_t": vs,
            "wout_t": wout_th, "boutv": bout,
            "wg": wgh, "wu": wuh, "wd": wdh,
            "cmask2": cmask, "identb": ident,
            "hselin": np.ascontiguousarray(
                np.broadcast_to(np.array([1.0 - h2, float(h2)], np.float32), (128, 2))),
            "exin": expander,
        })
    res = run_bass_kernel_spmd(nc, in_maps, core_ids=list(range(NCORE)))
    _RES_HOOK[0] = res
    outp = np.empty((B, T, D), dtype=np.float32)
    for c in range(NCORE):
        b, h2 = c // 2, c % 2
        outp[b, TPC * h2:TPC * h2 + TPC, :] = res.results[c]["out"]
    return outp
